# version 04
# baseline (speedup 1.0000x reference)
import numpy as np
from contextlib import ExitStack

import concourse.bass as bass
import concourse.mybir as mybir
from concourse.bass_utils import run_bass_kernel_spmd

F32 = mybir.dt.float32
BF16 = mybir.dt.bfloat16
AF = mybir.ActivationFunctionType
ALU = mybir.AluOpType
AX = mybir.AxisListType

NCORES = 8
B, S, D = 16, 2048, 1024
SPC = B // NCORES
T = SPC * S
NC = D // 128
NH = 16
INTER = 4096
NEXP = 8
EPS = 1e-5


class Sem:
    def __init__(self, nc, name):
        self.h = nc.alloc_semaphore(name)
        self.n = 0

    def inc(self, instr, k=1):
        instr.then_inc(self.h, k)
        self.n += k
        return instr

    def dma(self, eng, out, in_):
        return self.inc(eng.dma_start(out=out, in_=in_), 16)

    def wait(self, eng, n=None):
        eng.wait_ge(self.h, self.n if n is None else n)


def build_program(blocks=4):
    nc = bass.Bass("TRN2", target_bir_lowering=False, debug=False,
                   detect_race_conditions=False)

    x_in = nc.dram_tensor("x_in", [SPC, S, D], F32, kind="ExternalInput")
    out_t = nc.dram_tensor("out", [SPC, S, D], F32, kind="ExternalOutput")
    id32 = nc.dram_tensor("id32", [128, 128], F32, kind="ExternalInput")
    idbf = nc.dram_tensor("idbf", [128, 128], BF16, kind="ExternalInput")

    W = {}
    def win(name, shape):
        W[name] = nc.dram_tensor(name, list(shape), F32, kind="ExternalInput")
    for n in ("q", "k", "v", "o"):
        win(f"L0_attn_w_{n}", (D, D))
        win(f"L0_attn_s_{n}", (1,))
        win(f"L0_attn_b_{n}", (D,))
    for i, (a, b) in enumerate([(D, D), (D, 512), (512, 256), (256, 1)]):
        win(f"L0_ffn_cn_w{i+1}", (a, b))
        win(f"L0_ffn_cn_b{i+1}", (b,))
    for i, (a, b) in enumerate([(D, 512), (512, 256), (256, 1)]):
        win(f"L0_ffn_gc_w{i+1}", (a, b))
        win(f"L0_ffn_gc_b{i+1}", (b,))
    win("L0_ffn_adaptive_range", (1,))
    for n in ("Wq", "Wk", "Wv"):
        win(f"L1_attn_{n}", (NH, D, 64))
    win("L1_attn_Wp", (D, D))
    win("L1_attn_bp", (D,))
    win("L1_ffn_w1s", (D, INTER))
    win("L1_ffn_w2s", (D, INTER))
    win("L1_ffn_w3s", (INTER, D))
    win("L1_ffn_W1", (NEXP, D, INTER))
    win("L1_ffn_W2", (NEXP, D, INTER))
    win("L1_ffn_W3", (NEXP, INTER, D))
    win("L1_ffn_Wg", (D, NEXP))

    xr = nc.dram_tensor("xr", [128, NC, T], F32, kind="Internal")
    wt0 = nc.dram_tensor("wt0", [4, 128, NC, D], BF16, kind="Internal")

    es = ExitStack()
    sbuf = lambda name, shape, dt: es.enter_context(nc.sbuf_tensor(name, shape, dt))

    id32_sb = sbuf("id32_sb", [128, 128], F32)
    idbf_sb = sbuf("idbf_sb", [128, 128], BF16)
    ones1 = sbuf("ones1", [1, 128], F32)
    invD = sbuf("invD", [128, 1], BF16)
    sclrw = sbuf("sclrw", [1, 8], F32)
    scl = sbuf("scl", [128, 8], F32)
    bQ = sbuf("bQ", [128, NC], F32)
    bK = sbuf("bK", [128, NC], F32)
    bV = sbuf("bV", [128, NC], F32)
    bO = sbuf("bO", [128, NC], F32)
    bP = sbuf("bP", [128, NC], F32)
    bv_row = sbuf("bv_row", [1, D], F32)
    bp_zero = sbuf("bp_zero", [128, 1], F32)
    b1s = sbuf("b1s", [128, 8], F32)
    b2s = sbuf("b2s", [128, 4], F32)
    b3s = sbuf("b3s", [128, 2], F32)
    gb1 = sbuf("gb1", [128, 4], F32)
    gb2 = sbuf("gb2", [128, 2], F32)
    rowc = sbuf("rowc", [1, 8], F32)
    cgrow = sbuf("cgrow", [1, 2], F32)
    srow2 = sbuf("srow2", [1, 4], F32)

    blk = es.enter_context(nc.Block())

    _sems = {}
    def sem(name):
        if name not in _sems:
            _sems[name] = Sem(nc, name)
        return _sems[name]

    def dsem(name, k=2):
        return [sem(name + chr(97 + j)) for j in range(k)]

    pb = sem("pb")
    _nbar = [0]

    def barrier():
        _nbar[0] += 1
        tgt = 5 * _nbar[0]
        for en in ("tensor", "vector", "scalar", "sync", "gpsimd"):
            @getattr(blk, en)
            def _(e):
                e.drain().then_inc(pb.h, 1)
                e.wait_ge(pb.h, tgt)
        pb.n = tgt

    ld0 = sem("ld0"); g0 = sem("g0"); mm0 = sem("mm0"); ev0 = sem("ev0")

    @blk.sync
    def _(sy):
        es_nc = nc.allow_non_contiguous_dma("tiny one-time bias/scalar loads")
        es_nc.__enter__()
        ld0.dma(sy, id32_sb[:], id32[:])
        ld0.dma(sy, idbf_sb[:], idbf[:])
        for i, n in enumerate(("q", "k", "v", "o")):
            ld0.dma(sy, sclrw[0:1, i:i + 1], W[f"L0_attn_s_{n}"].ap()[None, :])
        ld0.dma(sy, sclrw[0:1, 4:5], W["L0_ffn_adaptive_range"].ap()[None, :])
        for dst, src in ((bQ, "L0_attn_b_q"), (bK, "L0_attn_b_k"),
                         (bV, "L0_attn_b_v"), (bO, "L0_attn_b_o"),
                         (bP, "L1_attn_bp")):
            ld0.dma(sy, dst[:], W[src].ap().rearrange("(c p) -> p c", p=128))
        ld0.dma(sy, bv_row[0:1, :], W["L0_attn_b_v"].ap()[None, :])
        ld0.dma(sy, b1s[:], W["L0_ffn_cn_b1"].ap().rearrange("(c p) -> p c", p=128))
        ld0.dma(sy, b2s[:], W["L0_ffn_cn_b2"].ap().rearrange("(c p) -> p c", p=128))
        ld0.dma(sy, b3s[:], W["L0_ffn_cn_b3"].ap().rearrange("(c p) -> p c", p=128))
        ld0.dma(sy, gb1[:], W["L0_ffn_gc_b1"].ap().rearrange("(c p) -> p c", p=128))
        ld0.dma(sy, gb2[:], W["L0_ffn_gc_b2"].ap().rearrange("(c p) -> p c", p=128))
        ld0.dma(sy, rowc[0:1, 0:1], W["L0_ffn_cn_b4"].ap()[None, :])
        ld0.dma(sy, rowc[0:1, 1:2], W["L0_ffn_gc_b3"].ap()[None, :])
        es_nc.__exit__(None, None, None)

    @blk.gpsimd
    def _(g):
        g.memset(ones1[:], 1.0)
        g.memset(bp_zero[:], 0.0)
        g.memset(rowc[0:1, 6:7], EPS)
        g0.inc(g.memset(invD[:], 1.0 / 1024.0))

    with nc.psum_tensor("p0scl", [128, 8], F32) as p0scl:
        @blk.tensor
        def _(t):
            ld0.wait(t)
            g0.wait(t)
            mm0.inc(t.matmul(p0scl[:, 0:5], ones1[:], sclrw[0:1, 0:5], start=True, stop=True))

        @blk.vector
        def _(v):
            mm0.wait(v)
            v.tensor_copy(scl[:, 0:5], p0scl[:, 0:5])
            v.tensor_scalar(rowc[0:1, 2:3], sclrw[0:1, 4:5], 0.3, None, ALU.mult)
            v.tensor_scalar(rowc[0:1, 3:4], rowc[0:1, 0:1], 0.5, None, ALU.mult)
            v.tensor_scalar(rowc[0:1, 4:5], rowc[0:1, 1:2], 0.5, None, ALU.mult)
            ev0.inc(v.tensor_scalar(rowc[0:1, 5:6], sclrw[0:1, 4:5], 0.7, None,
                                    ALU.mult))

    with (
        nc.sbuf_tensor("e_xt", [128, 2, D], F32) as e_xt,
        nc.sbuf_tensor("e_fm", [128, 2, D], F32) as e_fm,
        nc.psum_tensor("e_p", [128, 2, D], F32) as e_p,
    ):
        ldx = dsem("p0ldx"); mmx = sem("p0mmx"); evx = sem("p0evx"); stx = dsem("p0stx")
        ld_s, mm_s, ev_s, st_s = [], [], [], []
        for i in range(T // 128):
            bb, s0 = (i * 128) // S, (i * 128) % S

            @blk.sync
            def _(sy, i=i, bb=bb, s0=s0):
                if i >= 2:
                    mmx.wait(sy, mm_s[i - 2])
                ldx[i % 2].dma(sy, e_xt[:, i % 2], x_in[bb, s0:s0 + 128, :])
                ld_s.append(ldx[i % 2].n)

            @blk.tensor
            def _(t, i=i):
                if i == 0:
                    ev0.wait(t)
                t.wait_ge(ldx[i % 2].h, ld_s[i])
                if i >= 2:
                    evx.wait(t, ev_s[i - 2])
                for c in range(NC):
                    li = t.matmul(e_p[:, i % 2, c * 128:(c + 1) * 128],
                             e_xt[:, i % 2][:, c * 128:(c + 1) * 128], id32_sb[:],
                             start=(c % 4 == 0), stop=(c % 4 == 3),
                             skip_group_check=True)
                mmx.inc(li)
                mm_s.append(mmx.n)

            @blk.vector
            def _(v, i=i):
                v.wait_ge(mmx.h, mm_s[i])
                if i >= 2:
                    stx[i % 2].wait(v, st_s[i - 2])
                evx.inc(v.tensor_copy(e_fm[:, i % 2], e_p[:, i % 2]))
                ev_s.append(evx.n)

            @blk.scalar
            def _(s, i=i):
                s.wait_ge(evx.h, ev_s[i])
                stx[i % 2].dma(s, xr[:, :, i * 128:(i + 1) * 128],
                        e_fm[:, i % 2].rearrange("p (c t) -> p c t", c=NC))
                st_s.append(stx[i % 2].n)

        @blk.scalar
        def _(s):
            stx[0].wait(s)
            stx[1].wait(s)

    barrier()

    if blocks >= 1:
        with (
            nc.sbuf_tensor("w_tmp", [128, 2, NC, D], BF16) as w_tmp,
            nc.sbuf_tensor("w_tsb", [128, 2, D], BF16) as w_tsb,
            nc.psum_tensor("w_p", [128, 2, D], F32) as w_p,
        ):
            ldw = dsem("p0ldw"); mmw = sem("p0mmw"); evw = sem("p0evw"); stw = dsem("p0stw")
            lds, mms, evs, sts = [], [], [], []
            for w, nme in enumerate(("q", "k", "v", "o")):
                @blk.gpsimd
                def _(g, w=w, nme=nme):
                    if w >= 2:
                        mmw.wait(g, mms[(w - 1) * NC - 1])
                    ldw[w % 2].dma(g, w_tmp[:, w % 2],
                            W[f"L0_attn_w_{nme}"].ap().rearrange(
                                "(r p) h -> p r h", p=128))
                    lds.append(ldw[w % 2].n)

                for r in range(NC):
                    gi = w * NC + r

                    @blk.tensor
                    def _(t, w=w, r=r, gi=gi):
                        t.wait_ge(ldw[w % 2].h, lds[w])
                        if gi >= 2:
                            evw.wait(t, evs[gi - 2])
                        for c in range(NC):
                            li = t.matmul(w_p[:, gi % 2, c * 128:(c + 1) * 128],
                                     w_tmp[:, w % 2, r, c * 128:(c + 1) * 128],
                                     idbf_sb[:],
                                     start=(c % 4 == 0), stop=(c % 4 == 3),
                                     skip_group_check=True)
                        mmw.inc(li)
                        mms.append(mmw.n)

                    @blk.vector
                    def _(v, gi=gi):
                        v.wait_ge(mmw.h, mms[gi])
                        if gi >= 2:
                            stw[gi % 2].wait(v, sts[gi - 2])
                        evw.inc(v.tensor_copy(w_tsb[:, gi % 2], w_p[:, gi % 2]))
                        evs.append(evw.n)

                    @blk.scalar
                    def _(s, w=w, r=r, gi=gi):
                        s.wait_ge(evw.h, evs[gi])
                        stw[gi % 2].dma(s, wt0[w, :, :, r * 128:(r + 1) * 128],
                                w_tsb[:, gi % 2].rearrange("p (c o) -> p c o", c=NC))
                        sts.append(stw[gi % 2].n)

            @blk.scalar
            def _(s):
                stw[0].wait(s)
                stw[1].wait(s)

    barrier()

    _lnid = [0]

    def ln_pass(t0, t1, zb, zb_off_fn, store_rows=None, extra=None):
        _lnid[0] += 1
        pfx = "ln"
        px = dsem(pfx + "x"); pc = sem(pfx + "c"); pm = sem(pfx + "m")
        pq = sem(pfx + "q"); pv = sem(pfx + "v"); pr = sem(pfx + "r")
        pm2 = sem(pfx + "b"); pz = sem(pfx + "z")
        n = (t1 - t0) // 256
        ufx = "ln%d" % _lnid[0]
        with (
            nc.sbuf_tensor(ufx + "_xt", [128, 2, NC, 256], F32) as xt,
            nc.sbuf_tensor(ufx + "_xb", [128, NC, 256], BF16) as xb,
            nc.sbuf_tensor(ufx + "_xq", [128, NC, 256], BF16) as xq,
            nc.sbuf_tensor(ufx + "_zt", [128, 256], F32) as zt,
            nc.sbuf_tensor(ufx + "_rw", [1, 1024], F32) as rw,
            nc.psum_tensor(ufx + "_ps", [1, 2, 512], F32) as ps,
            nc.psum_tensor(ufx + "_pb", [128, 2, 512], F32) as pbx,
        ):
            px0 = [px[0].n, px[1].n]
            pc0, pm0, pq0, pv0, pr0, pb0, pz0 = (pc.n, pm.n, pq.n, pv.n,
                                                 pr.n, pm2.n, pz.n)
            zs = []
            for i in range(n):
                ta = t0 + i * 256

                @blk.sync
                def _(sy, i=i, ta=ta):
                    if i >= 2:
                        sy.wait_ge(pz.h, pz0 + i - 1)
                    px[i % 2].dma(sy, xt[:, i % 2], xr[:, :, ta:ta + 256])

                @blk.scalar
                def _(s, i=i):
                    s.wait_ge(px[i % 2].h, px0[i % 2] + 16 * (i // 2 + 1))
                    if i >= 1:
                        s.wait_ge(pm.h, pm0 + i)
                        s.wait_ge(pm2.h, pb0 + i)
                        if store_rows is not None:
                            s.wait_ge(pz.h, pz0 + i)
                    s.activation(xb[:], xt[:, i % 2], AF.Identity)
                    pc.inc(s.activation(xq[:], xt[:, i % 2], AF.Square))
                    s.wait_ge(pm.h, pm0 + i + 1)
                    s.activation(rw[0:1, 0:256], ps[0:1, i % 2, 0:256], AF.Identity)
                    pq.inc(s.activation(rw[0:1, 512:768], ps[0:1, i % 2, 0:256],
                                        AF.Square))
                    s.wait_ge(pv.h, pv0 + i + 1)
                    pr.inc(s.activation(rw[0:1, 768:1024], rw[0:1, 768:1024],
                                        AF.Ln, bias=rowc[0:1, 6:7]))
                    s.wait_ge(pr.h, pr.n)
                    pr.inc(s.activation(rw[0:1, 256:512], rw[0:1, 768:1024],
                                        AF.Exp, scale=-0.5))

                @blk.tensor
                def _(t, i=i):
                    t.wait_ge(pc.h, pc0 + i + 1)
                    if i >= 2:
                        t.wait_ge(pv.h, pv0 + i - 1)
                    for c in range(NC):
                        t.matmul(ps[0:1, i % 2, 0:256], invD[:], xb[:, c, :],
                                 start=(c == 0), stop=(c == NC - 1),
                                 skip_group_check=True)
                    for c in range(NC):
                        li = t.matmul(ps[0:1, i % 2, 256:512], invD[:], xq[:, c, :],
                                 start=False, stop=(c == NC - 1),
                                 skip_group_check=True)
                    pm.inc(li)
                    t.wait_ge(pr.h, pr0 + 2 * i + 2)
                    if i >= 2:
                        t.wait_ge(pz.h, pz0 + i - 1)
                    pm2.inc(t.matmul(pbx[:, i % 2], ones1[:], rw[0:1, 0:512],
                             start=True, stop=True, skip_group_check=True))

                @blk.vector
                def _(v, i=i, ta=ta):
                    v.wait_ge(pq.h, pq0 + i + 1)
                    pv.inc(v.tensor_tensor(rw[0:1, 768:1024],
                                           ps[0:1, i % 2, 256:512],
                                           rw[0:1, 512:768], ALU.subtract))
                    v.wait_ge(pm2.h, pb0 + i + 1)
                    zo = zb_off_fn(i, ta)
                    for c in range(NC):
                        v.tensor_tensor(zt[:], xt[:, i % 2][:, c, :],
                                        pbx[:, i % 2][:, 0:256], ALU.subtract)
                        v.tensor_tensor(zb[:, c, zo:zo + 256], zt[:],
                                        pbx[:, i % 2][:, 256:512], ALU.mult)
                    if store_rows is not None:
                        mrow, rrow = store_rows
                        v.tensor_copy(mrow[0:1, ta - t0:ta - t0 + 256],
                                      rw[0:1, 0:256])
                        v.tensor_copy(rrow[0:1, ta - t0:ta - t0 + 256],
                                      rw[0:1, 256:512])
                    pz.inc(v.drain())
                zs.append(pz0 + i + 1)

                if extra is not None:
                    extra(i, ta, zs[i], pz)

            for en in ("vector", "tensor", "scalar", "sync", "gpsimd"):
                @getattr(blk, en)
                def _(e):
                    pz.wait(e)
        return pz

    def attn_phase(layer):
        Lp = "a%d" % layer
        with (
            nc.sbuf_tensor(Lp + "_Q", [128, NC, S], BF16) as Qs,
            nc.sbuf_tensor(Lp + "_K", [128, NC, S], BF16) as Ks,
            nc.sbuf_tensor(Lp + "_V", [128, 16, NH, 65], BF16) as Vs,
            nc.sbuf_tensor(Lp + "_bvsb", [128, D], F32) as bv_sb,
            nc.sbuf_tensor(Lp + "_wo", [128, NC, D], BF16) as wo_sb,
        ):
            lwo = sem("aplwo%d" % layer)
            if layer == 0:
                @blk.sync
                def _(sy):
                    lwo.dma(sy, wo_sb[:], wt0[3])
            else:
                @blk.gpsimd
                def _(g):
                    wpv = W["L1_attn_Wp"].ap().rearrange(
                        "(e c a) o -> e c a o", e=64, c=8, a=2)
                    for aa in range(2):
                        lwo.dma(g, wo_sb[aa * 64:(aa + 1) * 64, :, :],
                                wpv[:, :, aa, :])
            lwo_snap = lwo.n

            if layer == 0:
                pvb = sem("apvb")
                with nc.psum_tensor(Lp + "_pvb", [128, D], F32) as p_vb:
                    @blk.tensor
                    def _(t):
                        ev0.wait(t)
                        for half in range(2):
                            li = t.matmul(p_vb[:, half * 512:(half + 1) * 512],
                                     ones1[:],
                                     bv_row[0:1, half * 512:(half + 1) * 512],
                                     start=True, stop=True,
                                     skip_group_check=True)
                        pvb.inc(li)

                    @blk.vector
                    def _(v):
                        pvb.wait(v)
                        pvb.inc(v.tensor_copy(bv_sb[:], p_vb[:]))

            gv = sem("apgv")

            @blk.gpsimd
            def _(g):
                gv.inc(g.memset(Vs[:, :, :, 64:65], 1.0))

            if layer == 0:
                @blk.tensor
                def _(t):
                    pvb.wait(t)

            for seq in range(SPC):
                t0 = seq * S
                with nc.sbuf_tensor(Lp + "_zb%d" % seq, [128, NC, S], BF16) as zb:
                    pz = ln_pass(t0, t0 + S, zb, lambda i, ta, t0=t0: ta - t0)
                    _proj(layer, seq, zb, Qs, Ks, Vs, bv_sb, pz, gv)
                _attn_core(layer, seq, t0, Qs, Ks, Vs, wo_sb, lwo, lwo_snap)
        barrier()

    def _proj(layer, seq, zb, Qs, Ks, Vs, bv_sb, pz, gv):
        Lp = "a%dp%d" % (layer, seq)
        lw = dsem("prlw%d" % layer); pmm = sem("prmm"); pev = sem("prev")
        with (
            nc.sbuf_tensor(Lp + "_w", [128, 2, NC, D], BF16) as w_sb,
            nc.psum_tensor(Lp + "_pp", [128, 2, 512], F32) as pp,
        ):
            lws = []
            wdone = []
            evl = []
            slot = [0]

            for wi in range(3):
                eng = "sync" if layer == 0 else "gpsimd"

                @getattr(blk, eng)
                def _(e, wi=wi):
                    if wi >= 2:
                        pmm.wait(e, wdone[wi - 2])
                    if layer == 0:
                        lw[wi % 2].dma(e, w_sb[:, wi % 2], wt0[wi])
                    else:
                        nm = ("Wq", "Wk", "Wv")[wi]
                        wv4 = W[f"L1_attn_{nm}"].ap().rearrange(
                            "h (c p) e -> c p h e", p=128)
                        for c in range(NC):
                            lw[wi % 2].dma(
                                e,
                                w_sb[:, wi % 2, c, :].rearrange(
                                    "p (h e) -> p h e", e=64),
                                wv4[c])
                    lws.append(lw[wi % 2].n)

                if wi < 2:
                    dst = (Qs, Ks)[wi]
                    for ts in range(S // 512):
                        for m in range(NC):
                            j = slot[0]

                            @blk.tensor
                            def _(t, wi=wi, ts=ts, m=m, j=j):
                                t.wait_ge(lw[wi % 2].h, lws[wi])
                                pz.wait(t)
                                if j >= 2:
                                    pev.wait(t, evl[j - 2])
                                for c in range(NC):
                                    li = t.matmul(pp[:, j % 2],
                                             w_sb[:, wi % 2][:, c, m * 128:(m + 1) * 128],
                                             zb[:, c, ts * 512:(ts + 1) * 512],
                                             start=(c == 0), stop=(c == NC - 1))
                                pmm.inc(li)

                            @blk.vector
                            def _(v, wi=wi, ts=ts, m=m, j=j):
                                pmm.wait(v)
                                dd = dst[:, m, ts * 512:(ts + 1) * 512]
                                if layer == 0:
                                    pev.inc(v.tensor_scalar(
                                        dd, pp[:, j % 2], scl[:, wi:wi + 1],
                                        (bQ, bK)[wi][:, m:m + 1],
                                        ALU.mult, ALU.add))
                                else:
                                    pev.inc(v.tensor_copy(dd, pp[:, j % 2]))
                                evl.append(pev.n)

                            slot[0] += 1
                    wdone.append(pmm.n)
                else:
                    for tk in range(S // 128):
                        for vc in range(2):
                            j = slot[0]

                            @blk.tensor
                            def _(t, tk=tk, vc=vc, j=j):
                                t.wait_ge(lw[0].h, lws[2])
                                pz.wait(t)
                                gv.wait(t)
                                if j >= 2:
                                    pev.wait(t, evl[j - 2])
                                for c in range(NC):
                                    li = t.matmul(pp[:, j % 2],
                                             zb[:, c, tk * 128:(tk + 1) * 128],
                                             w_sb[:, 0, c, vc * 512:(vc + 1) * 512],
                                             start=(c == 0), stop=(c == NC - 1))
                                pmm.inc(li)

                            @blk.vector
                            def _(v, layer=layer, seq=seq, tk=tk, vc=vc, j=j):
                                pmm.wait(v)
                                dsl = Vs[:, tk, vc * 8:(vc + 1) * 8, 0:64]
                                src = pp[:, j % 2].rearrange("p (h e) -> p h e", e=64)
                                if layer == 0:
                                    bvv = bv_sb[:, vc * 512:(vc + 1) * 512] \
                                        .rearrange("p (h e) -> p h e", e=64)
                                    v.scalar_tensor_tensor(dsl, src, scl[:, 2:3],
                                                           bvv, ALU.mult, ALU.add)
                                else:
                                    v.tensor_copy(dsl, src)
                                pev.inc(v.drain())
                                evl.append(pev.n)

                            slot[0] += 1
                    wdone.append(pmm.n)

            for en in ("tensor", "vector", "scalar", "sync", "gpsimd"):
                @getattr(blk, en)
                def _(e):
                    pev.wait(e)

    def _attn_core(layer, seq, t0, Qs, Ks, Vs, wo_sb, lwo, lwo_snap):
        Lp = "a%dc%d" % (layer, seq)
        amm = sem("acmm"); aex = sem("acex"); acp = sem("accp")
        arw = sem("acrw"); abc = sem("acbc"); aev = sem("acev")
        aom = sem("acom"); alr = dsem("aclr"); aoe = sem("acoe")
        axv = sem("acxv"); axs = dsem("acxs")
        with (
            nc.sbuf_tensor(Lp + "_Pp", [128, 2, 1024], BF16) as Pp,
            nc.sbuf_tensor(Lp + "_ctx", [128, NC, 512], BF16) as ctx,
            nc.sbuf_tensor(Lp + "_rin", [64, 2, 512], F32) as rin,
            nc.sbuf_tensor(Lp + "_srow", [1, 1024], F32) as srow,
            nc.sbuf_tensor(Lp + "_xrt", [128, 2, NC, 512], F32) as xrt,
            nc.sbuf_tensor(Lp + "_otmp", [128, 2, 512], F32) as otmp,
            nc.psum_tensor(Lp + "_st", [128, 2, 1024], F32) as stp,
            nc.psum_tensor(Lp + "_cx", [65, 2, 512], F32) as cxp,
            nc.psum_tensor(Lp + "_rb", [64, 512], F32) as rbp,
            nc.psum_tensor(Lp + "_po", [128, 512], F32) as pop,
        ):
            rw_snap = {}; bc_snap = {}; rcopy = {}; ev_done = {}; bc_plan = {}
            om_snap = []; oe_snap = []; xv_snap = []; st_snap = []; lr_snap = []
            pend_bc = [None]
            for qb in range(4):
                @blk.sync
                def _(sy, qb=qb):
                    if qb >= 2:
                        axs[qb % 2].wait(sy, st_snap[qb - 2])
                    alr[qb % 2].dma(sy, xrt[:, qb % 2],
                            xr[:, :, t0 + qb * 512: t0 + (qb + 1) * 512])
                    lr_snap.append(alr[qb % 2].n)

                for h in range(NH):
                    hp, ho = h // 2, (h % 2) * 64
                    gidx = qb * NH + h
                    amm0, aex0, acp0 = amm.n, aex.n, acp.n

                    @blk.tensor
                    def _(t, qb=qb, h=h, hp=hp, ho=ho, gidx=gidx,
                          aex0=aex0, acp0=acp0):
                        for g in range(8):
                            if gidx * 8 + g >= 2:
                                t.wait_ge(aex.h, aex0 + g - 1 if g >= 2
                                          else ex_tail[gidx - 1] - (1 - g))
                            for kt in (2 * g, 2 * g + 1):
                                li = t.matmul(
                                    stp[:, g % 2, (kt % 2) * 512:(kt % 2 + 1) * 512],
                                    Ks[ho:ho + 64, hp, kt * 128:(kt + 1) * 128],
                                    Qs[ho:ho + 64, hp, qb * 512:(qb + 1) * 512],
                                    start=True, stop=True, skip_group_check=True)
                            amm.inc(li)
                            if g >= 1:
                                t.wait_ge(aex.h, aex0 + g)
                                if g == 1 and gidx >= 2:
                                    aev.wait(t, ev_done[gidx - 2])
                                for kt in (2 * g - 2, 2 * g - 1):
                                    li = t.matmul(cxp[:, gidx % 2], Vs[:, kt, h, :],
                                             Pp[:, (g - 1) % 2, (kt % 2) * 512:(kt % 2 + 1) * 512],
                                             start=(kt == 0), stop=False,
                                             skip_group_check=True)
                                acp.inc(li)
                        t.wait_ge(aex.h, aex0 + 8)
                        for kt in (14, 15):
                            li = t.matmul(cxp[:, gidx % 2], Vs[:, kt, h, :],
                                     Pp[:, 1, (kt % 2) * 512:(kt % 2 + 1) * 512],
                                     start=False, stop=(kt == 15),
                                     skip_group_check=True)
                        acp.inc(li)
                        if pend_bc[0] is not None:
                            pend_bc[0](t)
                            pend_bc[0] = None
                    assert amm.n == amm0 + 8 and acp.n == acp0 + 8

                    @blk.scalar
                    def _(s, gidx=gidx, amm0=amm0, aex0=aex0, acp0=acp0):
                        for g in range(8):
                            s.wait_ge(amm.h, amm0 + g + 1)
                            if gidx * 8 + g >= 2:
                                s.wait_ge(acp.h, acp0 + g - 1 if g >= 2
                                          else cp_tail[gidx - 1] - (1 - g))
                            aex.inc(s.activation(Pp[:, g % 2], stp[:, g % 2],
                                                 AF.Exp, scale=0.125))
                        s.wait_ge(acp.h, acp0 + 8)
                        if gidx >= 1:
                            abc.wait(s, bc_snap[gidx - 1])
                        arw.inc(s.activation(srow[0:1, 0:512],
                                             cxp[64:65, gidx % 2], AF.Ln))
                        s.wait_ge(arw.h, arw.n)
                        arw.inc(s.activation(srow[0:1, 512:1024],
                                             srow[0:1, 0:512], AF.Exp,
                                             scale=-1.0))
                        rw_snap[gidx] = arw.n
                    ex_tail = getattr(_attn_core, "_ex_tail", {})
                    ex_tail[gidx] = aex0 + 8
                    _attn_core._ex_tail = ex_tail
                    cp_tail = getattr(_attn_core, "_cp_tail", {})
                    cp_tail[gidx] = acp0 + 8
                    _attn_core._cp_tail = cp_tail

                    def mk_bc(gidx=gidx):
                        def emit(t):
                            t.wait_ge(arw.h, rw_snap[gidx])
                            if gidx >= 1:
                                aev.wait(t, rcopy[gidx - 1])
                            abc.inc(t.matmul(rbp[:], ones1[0:1, 0:64],
                                     srow[0:1, 512:1024], start=True, stop=True,
                                     skip_group_check=True))
                            bc_snap[gidx] = abc.n
                        return emit
                    pend_bc[0] = mk_bc()
                    bc_plan[gidx] = abc.n + 1

                    @blk.vector
                    def _(v, qb=qb, hp=hp, ho=ho, gidx=gidx):
                        v.wait_ge(abc.h, bc_plan[gidx])
                        if qb >= 1 and gidx % NH == 0:
                            aom.wait(v, om_snap[qb - 1])
                        v.tensor_copy(rin[:, gidx % 2], rbp[:])
                        aev.inc(v.drain())
                        rcopy[gidx] = aev.n
                        v.wait_ge(aev.h, aev.n)
                        aev.inc(v.tensor_tensor(ctx[ho:ho + 64, hp, :],
                                                cxp[0:64, gidx % 2],
                                                rin[:, gidx % 2], ALU.mult))
                        ev_done[gidx] = aev.n

                @blk.tensor
                def _(t):
                    if pend_bc[0] is not None:
                        pend_bc[0](t)
                        pend_bc[0] = None
                for m in range(NC):
                    j = qb * NC + m

                    @blk.tensor
                    def _(t, qb=qb, m=m, j=j):
                        t.wait_ge(lwo.h, lwo_snap)
                        aev.wait(t)
                        if j >= 1:
                            aoe.wait(t, oe_snap[j - 1])
                        for c in range(NC):
                            li = t.matmul(pop[:],
                                     wo_sb[:, c, m * 128:(m + 1) * 128],
                                     ctx[:, c, :],
                                     start=(c == 0), stop=(c == NC - 1))
                        aom.inc(li)
                        if m == NC - 1:
                            om_snap.append(aom.n)

                    @blk.vector
                    def _(v, qb=qb, m=m, j=j):
                        aom.wait(v)
                        if j >= 2:
                            axv.wait(v, xv_snap[j - 2])
                        if layer == 0:
                            bias, scale = bO[:, m:m + 1], scl[:, 3:4]
                        else:
                            bias, scale = bP[:, m:m + 1], 1.0
                        aoe.inc(v.tensor_scalar(otmp[:, j % 2], pop[:],
                                                scale, bias, ALU.mult, ALU.add))
                        oe_snap.append(aoe.n)
                        v.wait_ge(alr[qb % 2].h, lr_snap[qb])
                        axv.inc(v.tensor_tensor(xrt[:, qb % 2, m, :],
                                                xrt[:, qb % 2, m, :],
                                                otmp[:, j % 2], ALU.add))
                        xv_snap.append(axv.n)

                @blk.scalar
                def _(s, qb=qb):
                    axv.wait(s)
                    axs[qb % 2].dma(s, xr[:, :, t0 + qb * 512: t0 + (qb + 1) * 512],
                            xrt[:, qb % 2])
                    st_snap.append(axs[qb % 2].n)

            for en in ("scalar", "vector", "tensor", "sync"):
                @getattr(blk, en)
                def _(e):
                    axs[0].wait(e)
                    axs[1].wait(e)

    def p2_phase():
        with (
            nc.sbuf_tensor("p2_mrow", [1, T], F32) as mrow,
            nc.sbuf_tensor("p2_rrow", [1, T], F32) as rrow,
            nc.sbuf_tensor("p2_zbt", [128, NC, 512], BF16) as zbt,
            nc.sbuf_tensor("p2_gacc", [128, NC, 2], F32) as gacc,
            nc.sbuf_tensor("p2_gtmp", [128, NC], F32) as gtmp,
            nc.sbuf_tensor("p2_cn1", [128, NC, D], BF16) as cn1,
            nc.sbuf_tensor("p2_cn2", [128, NC, 512], BF16) as cn2,
            nc.sbuf_tensor("p2_cn3", [128, 4, 256], BF16) as cn3,
            nc.sbuf_tensor("p2_cn4", [128, 2, 1], BF16) as cn4,
            nc.sbuf_tensor("p2_gc1", [128, NC, 512], BF16) as gc1,
            nc.sbuf_tensor("p2_gc2", [128, 4, 256], BF16) as gc2,
            nc.sbuf_tensor("p2_gc3", [128, 2, 1], BF16) as gc3,
        ):
            lw2 = sem("p2lw"); gz = sem("p2gz"); gv2 = sem("p2gv")

            @blk.gpsimd
            def _(g):
                gz.inc(g.memset(gacc[:], 0.0))
                with nc.allow_non_contiguous_dma("tiny [.,1] weight columns"):
                    for nm, dst in (("cn_w1", cn1), ("cn_w2", cn2),
                                    ("cn_w3", cn3), ("cn_w4", cn4),
                                    ("gc_w1", gc1), ("gc_w2", gc2),
                                    ("gc_w3", gc3)):
                        lw2.dma(g, dst[:], W["L0_ffn_" + nm].ap().rearrange(
                            "(c p) o -> p c o", p=128))

            def gc_extra(i, ta, zsnap, pz):
                @blk.vector
                def _(v, i=i, ta=ta, zsnap=zsnap):
                    gz.wait(v)
                    sq = ta // S
                    off = (i % 2) * 256
                    v.tensor_reduce(gtmp[:], zbt[:, :, off:off + 256],
                                    AX.X, ALU.add)
                    gv2.inc(v.tensor_tensor(gacc[:, :, sq], gacc[:, :, sq],
                                            gtmp[:], ALU.add))

            ln_pass(0, T, zbt, lambda i, ta: (i % 2) * 256,
                    store_rows=(mrow, rrow), extra=gc_extra)

            gmm = sem("p2gmm"); gev = sem("p2gev")
            with (
                nc.sbuf_tensor("p2_gcb", [128, NC, 2], BF16) as gcb,
                nc.sbuf_tensor("p2_g1", [128, 4, 2], BF16) as g1b,
                nc.sbuf_tensor("p2_g2", [128, 2, 2], BF16) as g2b,
                nc.psum_tensor("p2_pg", [128, 8], F32) as pg,
                nc.psum_tensor("p2_pg2", [128, 4], F32) as pg2,
                nc.psum_tensor("p2_pg3", [1, 2], F32) as pg3,
            ):
                @blk.vector
                def _(v):
                    gv2.wait(v)
                    gev.inc(v.tensor_scalar(gcb[:], gacc[:], 1.0 / S, None,
                                            ALU.mult))

                @blk.tensor
                def _(t):
                    gev.wait(t)
                    lw2.wait(t)
                    for m in range(4):
                        for c in range(NC):
                            li = t.matmul(pg[:, m * 2:(m + 1) * 2],
                                     gc1[:, c, m * 128:(m + 1) * 128],
                                     gcb[:, c, :],
                                     start=(c == 0 and m == 0),
                                     stop=(c == NC - 1 and m == 3),
                                     skip_group_check=True)
                    gmm.inc(li)

                @blk.scalar
                def _(s):
                    gmm.wait(s)
                    for m in range(4):
                        s.activation(g1b[:, m, :], pg[:, m * 2:(m + 1) * 2],
                                     AF.Gelu, bias=gb1[:, m:m + 1])
                    gev.inc(s.drain())

                @blk.tensor
                def _(t):
                    gev.wait(t)
                    for m in range(2):
                        for c in range(4):
                            li = t.matmul(pg2[:, m * 2:(m + 1) * 2],
                                     gc2[:, c, m * 128:(m + 1) * 128],
                                     g1b[:, c, :],
                                     start=(c == 0 and m == 0),
                                     stop=(c == 3 and m == 1),
                                     skip_group_check=True)
                    gmm.inc(li)

                @blk.scalar
                def _(s):
                    gmm.wait(s)
                    for m in range(2):
                        s.activation(g2b[:, m, :], pg2[:, m * 2:(m + 1) * 2],
                                     AF.Gelu, bias=gb2[:, m:m + 1])
                    gev.inc(s.drain())

                @blk.tensor
                def _(t):
                    gev.wait(t)
                    for c in range(2):
                        li = t.matmul(pg3[:], gc3[:, c, :], g2b[:, c, :],
                                 start=(c == 0), stop=(c == 1),
                                 skip_group_check=True)
                    gmm.inc(li)

                @blk.scalar
                def _(s):
                    gmm.wait(s)
                    gev.inc(s.activation(srow2[0:1, 0:2], pg3[0:1, :], AF.Tanh,
                                         scale=0.5, bias=rowc[0:1, 4:5]))

                @blk.vector
                def _(v):
                    gev.wait(v)
                    gev.inc(v.tensor_scalar(cgrow[0:1, :], srow2[0:1, 0:2],
                                            rowc[0:1, 5:6], 0.1,
                                            ALU.mult, ALU.add))

            cx = dsem("p2cx"); cb = sem("p2cb"); cmm = sem("p2cmm")
            cge = sem("p2cge"); ctf = sem("p2ctf"); ccr = sem("p2ccr")
            cbc = sem("p2cbc"); cup = sem("p2cup"); cst = dsem("p2cst")
            cca = sem("p2cca")
            SGN = [1.0, 1.0, -1.0, -1.0, -1.0, -1.0, 1.0, 1.0]
            PRT = [4, 5, 6, 7, 0, 1, 2, 3]
            es2 = ExitStack()
            xrt = es2.enter_context(nc.sbuf_tensor("p2_xrt", [128, 2, NC, 512], F32))
            oA = es2.enter_context(nc.sbuf_tensor("p2_oA", [128, NC, 512], F32))
            oB = es2.enter_context(nc.sbuf_tensor("p2_oB", [128, NC, 512], F32))
            ob = es2.enter_context(nc.sbuf_tensor("p2_ob", [128, NC, 512], BF16))
            u1b = es2.enter_context(nc.sbuf_tensor("p2_u1", [128, NC, 512], BF16))
            u2b = es2.enter_context(nc.sbuf_tensor("p2_u2", [128, 4, 512], BF16))
            u3b = es2.enter_context(nc.sbuf_tensor("p2_u3", [128, 2, 512], BF16))
            csb = es2.enter_context(nc.sbuf_tensor("p2_csb", [128, 512], F32))
            ztm = es2.enter_context(nc.sbuf_tensor("p2_ztm", [128, 512], F32))
            trow = es2.enter_context(nc.sbuf_tensor("p2_trow", [1, 512], F32))
            crow = es2.enter_context(nc.sbuf_tensor("p2_crow", [1, 512], F32))
            pb2 = es2.enter_context(nc.psum_tensor("p2_pb2", [128, 1024], F32))
            pu = es2.enter_context(nc.psum_tensor("p2_pu", [128, 2, 512], F32))
            p4 = es2.enter_context(nc.psum_tensor("p2_p4", [1, 512], F32))
            pcr = es2.enter_context(nc.psum_tensor("p2_pcr", [128, 512], F32))
            if True:
                xs2, st2 = [], []

                @blk.tensor
                def _(t):
                    gev.wait(t)
                @blk.sync
                def _(sy):
                    gev.wait(sy)

                def dense_layer(win_sb, bsb, uout, nm_, nk_, cycsem):
                    c0, g0 = cmm.n, cge.n

                    @blk.tensor
                    def _(t):
                        cge.wait(t, g0)
                        for m in range(nm_):
                            if m >= 2:
                                t.wait_ge(cge.h, g0 + m - 1)
                            for c in range(nk_):
                                li = t.matmul(pu[:, m % 2],
                                         win_sb[:, c, m * 128:(m + 1) * 128],
                                         cycsem[0][:, c, :],
                                         start=(c == 0), stop=(c == nk_ - 1))
                            cmm.inc(li)

                    @blk.scalar
                    def _(s):
                        for m in range(nm_):
                            s.wait_ge(cmm.h, c0 + m + 1)
                            cge.inc(s.activation(uout[:, m, :], pu[:, m % 2],
                                                 AF.Gelu, bias=bsb[:, m:m + 1]))

                for ti in range(T // 512):
                    ta = ti * 512
                    sq = ta // S

                    @blk.sync
                    def _(sy, ti=ti, ta=ta):
                        if ti >= 2:
                            cst[ti % 2].wait(sy, st2[ti - 2])
                        cx[ti % 2].dma(sy, xrt[:, ti % 2], xr[:, :, ta:ta + 512])
                        xs2.append(cx[ti % 2].n)

                    @blk.tensor
                    def _(t, ti=ti, ta=ta):
                        t.wait_ge(cx[ti % 2].h, xs2[ti])
                        cca.wait(t)
                        t.matmul(pb2[:, 0:512], ones1[:], mrow[0:1, ta:ta + 512],
                                 start=True, stop=True, skip_group_check=True)
                        cb.inc(t.matmul(pb2[:, 512:1024], ones1[:],
                                 rrow[0:1, ta:ta + 512],
                                 start=True, stop=True, skip_group_check=True))

                    @blk.vector
                    def _(v, ti=ti):
                        cb.wait(v)
                        for c in range(NC):
                            v.tensor_tensor(ztm[:], xrt[:, ti % 2, c, :],
                                            pb2[:, 0:512], ALU.subtract)
                            v.tensor_tensor(oA[:, c, :], ztm[:],
                                            pb2[:, 512:1024], ALU.mult)
                        cup.inc(v.drain())

                    @blk.scalar
                    def _(s, ti=ti):
                        cup.wait(s)
                        for c in range(NC):
                            s.activation(ob[:, c, :], oA[:, c, :], AF.Identity)
                        cge.inc(s.drain())

                    for cyc in range(3):
                        src, dst = (oA, oB) if cyc % 2 == 0 else (oB, oA)
                        dense_layer(cn1, b1s, u1b, NC, NC, [ob])
                        dense_layer(cn2, b2s, u2b, 4, NC, [u1b])
                        dense_layer(cn3, b3s, u3b, 2, 4, [u2b])

                        @blk.tensor
                        def _(t):
                            cge.wait(t)
                            for c in range(2):
                                li = t.matmul(p4[:], cn4[:, c, :], u3b[:, c, :],
                                         start=(c == 0), stop=(c == 1),
                                         skip_group_check=True)
                            cmm.inc(li)

                        @blk.scalar
                        def _(s):
                            cmm.wait(s)
                            ctf.inc(s.activation(trow[0:1, :], p4[0:1, :],
                                                 AF.Tanh, scale=0.5,
                                                 bias=rowc[0:1, 3:4]))

                        @blk.vector
                        def _(v, sq=sq):
                            ctf.wait(v)
                            ccr.inc(v.tensor_scalar(crow[0:1, :], trow[0:1, :],
                                                    rowc[0:1, 2:3],
                                                    cgrow[0:1, sq:sq + 1],
                                                    ALU.mult, ALU.add))

                        @blk.tensor
                        def _(t):
                            ccr.wait(t)
                            cbc.inc(t.matmul(pcr[:], ones1[:], crow[0:1, :],
                                     start=True, stop=True,
                                     skip_group_check=True))

                        @blk.vector
                        def _(v, src=src, dst=dst):
                            cbc.wait(v)
                            v.tensor_copy(csb[:], pcr[:])
                            cup.inc(v.drain())
                            v.wait_ge(cup.h, cup.n)
                            for c in range(NC):
                                v.tensor_tensor(ztm[:], csb[:],
                                                src[:, PRT[c], :], ALU.mult)
                                v.scalar_tensor_tensor(dst[:, c, :], ztm[:],
                                                       SGN[c], src[:, c, :],
                                                       ALU.mult, ALU.add)
                            cup.inc(v.drain())

                        if cyc < 2:
                            @blk.scalar
                            def _(s, dst=dst):
                                cup.wait(s)
                                for c in range(NC):
                                    s.activation(ob[:, c, :], dst[:, c, :],
                                                 AF.Identity)
                                cge.inc(s.drain())

                    @blk.vector
                    def _(v, ti=ti):
                        cup.wait(v)
                        for c in range(NC):
                            v.tensor_tensor(xrt[:, ti % 2, c, :],
                                            xrt[:, ti % 2, c, :],
                                            oB[:, c, :], ALU.add)
                        cca.inc(v.drain())

                    @blk.scalar
                    def _(s, ti=ti, ta=ta):
                        cca.wait(s)
                        cst[ti % 2].dma(s, xr[:, :, ta:ta + 512], xrt[:, ti % 2])
                        st2.append(cst[ti % 2].n)

                for en in ("scalar", "vector", "tensor", "sync"):
                    @getattr(blk, en)
                    def _(e):
                        cst[0].wait(e)
                        cst[1].wait(e)
            es2.close()
        barrier()

    def p4_phase(do_ffn):
        QT = 1024
        for q in range(T // QT):
            t0 = q * QT
            pfx = "p4q%d" % q
            with (
                nc.sbuf_tensor(pfx + "_zb", [128, NC, QT], BF16) as zbq,
                nc.sbuf_tensor(pfx + "_msk", [128, 8, NEXP], F32) as msk,
                nc.sbuf_tensor(pfx + "_ysb", [128, 8, D], BF16) as ysb,
            ):
                glg = sem("p4lg")
                if do_ffn:
                    with (
                        nc.sbuf_tensor(pfx + "_wg", [128, NC, 8], BF16) as wg_sb,
                        nc.sbuf_tensor(pfx + "_mx", [128, 1], F32) as mx,
                        nc.psum_tensor(pfx + "_pgt", [128, 2, 512], F32) as pgt,
                    ):
                        gmx = sem("p4mx"); gmk = sem("p4mk")
                        mk_snap = []

                        @blk.gpsimd
                        def _(g):
                            glg.dma(g, wg_sb[:],
                                    W["L1_ffn_Wg"].ap().rearrange(
                                        "(c p) e -> p c e", p=128))
                        glg_snap = glg.n

                        def gate_extra(i, ta, zsnap, pz):
                            for tk2 in range(2):
                                tko = (ta - t0) // 128 + tk2

                                @blk.tensor
                                def _(t, ta=ta, tk2=tk2, tko=tko, zsnap=zsnap, pz=pz):
                                    t.wait_ge(glg.h, glg_snap)
                                    t.wait_ge(pz.h, zsnap)
                                    if tko >= 2:
                                        gmk.wait(t, mk_snap[tko - 2])
                                    o = (ta - t0) + tk2 * 128
                                    for c in range(NC):
                                        li = t.matmul(pgt[:, tko % 2, 0:8],
                                                 zbq[:, c, o:o + 128],
                                                 wg_sb[:, c, :],
                                                 start=(c == 0),
                                                 stop=(c == NC - 1))
                                    gmx.inc(li)

                                @blk.vector
                                def _(v, tko=tko):
                                    gmx.wait(v)
                                    v.reduce_max(mx[:], pgt[:, tko % 2, 0:8],
                                                 axis=AX.X)
                                    gmk.inc(v.drain())
                                    v.wait_ge(gmk.h, gmk.n)
                                    for e in range(NEXP):
                                        v.tensor_tensor(msk[:, tko, e:e + 1],
                                                        pgt[:, tko % 2, e:e + 1],
                                                        mx[:], ALU.is_ge)
                                    gmk.inc(v.drain())
                                    mk_snap.append(gmk.n)

                        ln_pass(t0, t0 + QT, zbq, lambda i, ta, t0=t0: ta - t0,
                                extra=gate_extra)
                    _experts(q, t0, pfx, zbq, msk, ysb)
                else:
                    @blk.gpsimd
                    def _(g):
                        glg.inc(g.memset(ysb[:], 0.0))

                _exit_quarter(q, t0, pfx, ysb, do_ffn, glg)
        barrier()

    def _experts(q, t0, pfx, zbq, msk, ysb):
        lw = dsem("exl"); lw3 = dsem("ex3"); umm = sem("exu"); uge = sem("exg")
        vev = sem("exv"); ymm = sem("exm"); yev = sem("exy")
        gmx_f, gmk_f = sem("p4mx"), sem("p4mk")
        for en in ("gpsimd", "tensor", "sync"):
            @getattr(blk, en)
            def _(e):
                gmx_f.wait(e)
                gmk_f.wait(e)
        with (
            nc.sbuf_tensor(pfx + "_w1", [128, 2, NC, 512], BF16) as w1s,
            nc.sbuf_tensor(pfx + "_w2", [128, 2, NC, 512], BF16) as w2s,
            nc.sbuf_tensor(pfx + "_w3", [128, 2, 32, 512], BF16) as w3s,
            nc.sbuf_tensor(pfx + "_hb", [128, 32, 1024], BF16) as hb,
            nc.sbuf_tensor(pfx + "_gu", [128, 4, 512], BF16) as gub,
            nc.psum_tensor(pfx + "_pu", [128, 2, 512], F32) as puP,
            nc.psum_tensor(pfx + "_pv", [128, 2, 512], F32) as pvP,
            nc.psum_tensor(pfx + "_py", [128, 2, 512], F32) as pyP,
        ):
            w_snap = {}; w3_snap = {}; um_snap = []; ug_snap = []
            hv_snap = []; ym_snap = []; yv_snap = []
            for e in range(NEXP + 1):
                def w12src(mat, s8, e=e):
                    nm = ("L1_ffn_w1s", "L1_ffn_w2s")[mat] if e == 0 else \
                         ("L1_ffn_W1", "L1_ffn_W2")[mat]
                    ap = W[nm].ap() if e == 0 else W[nm][e - 1]
                    return ap.rearrange("(c p) o -> p c o", p=128)[
                        :, :, s8 * 512:(s8 + 1) * 512]

                def w3src(dq, e=e):
                    nm = "L1_ffn_w3s" if e == 0 else "L1_ffn_W3"
                    ap = W[nm].ap() if e == 0 else W[nm][e - 1]
                    return ap.rearrange("(rc p) o -> p rc o", p=128)[
                        :, :, dq * 512:(dq + 1) * 512]

                for s8 in range(NC):
                    k = e * NC + s8

                    @blk.gpsimd
                    def _(g, s8=s8, k=k, e=e):
                        if k >= 2:
                            umm.wait(g, um_snap[(k - 1) * 8 - 1] if k >= 2
                                     else 0)
                        lw[s8 % 2].dma(g, w1s[:, s8 % 2], w12src(0, s8))
                        lw[s8 % 2].dma(g, w2s[:, s8 % 2], w12src(1, s8))
                        w_snap[k] = lw[s8 % 2].n

                    u0, g0, v0 = umm.n, uge.n, vev.n
                    jb = k * 8

                    @blk.tensor
                    def _(t, s8=s8, k=k, e=e, u0=u0, g0=g0, v0=v0, jb=jb):
                        t.wait_ge(lw[s8 % 2].h, w_snap[k])
                        if e > 0 and s8 == 0:
                            ymm.wait(t, ym_snap[-1])
                        for tt in range(2):
                            for m in range(4):
                                j = tt * 4 + m
                                if jb + j >= 2:
                                    t.wait_ge(uge.h, g0 + j - 1)
                                    t.wait_ge(vev.h, v0 + j - 1)
                                for c in range(NC):
                                    li = t.matmul(puP[:, m % 2],
                                             w1s[:, s8 % 2, c, m * 128:(m + 1) * 128],
                                             zbq[:, c, tt * 512:(tt + 1) * 512],
                                             start=(c == 0), stop=(c == NC - 1))
                                umm.inc(li)
                                for c in range(NC):
                                    li = t.matmul(pvP[:, m % 2],
                                             w2s[:, s8 % 2, c, m * 128:(m + 1) * 128],
                                             zbq[:, c, tt * 512:(tt + 1) * 512],
                                             start=(c == 0), stop=(c == NC - 1))
                                umm.inc(li)
                    for _j in range(8):
                        um_snap.append(u0 + 2 * _j + 2)
                    assert umm.n == u0 + 16

                    @blk.scalar
                    def _(s, u0=u0, v0=v0, jb=jb):
                        for tt in range(2):
                            for m in range(4):
                                j = tt * 4 + m
                                s.wait_ge(umm.h, u0 + 2 * j + 1)
                                if jb + j >= 2:
                                    s.wait_ge(vev.h, v0 + j - 1)
                                uge.inc(s.activation(gub[:, m, :], puP[:, m % 2],
                                                     AF.Gelu))
                    assert uge.n == g0 + 8

                    @blk.vector
                    def _(v, s8=s8, u0=u0, g0=g0):
                        for tt in range(2):
                            for m in range(4):
                                j = tt * 4 + m
                                v.wait_ge(uge.h, g0 + j + 1)
                                v.wait_ge(umm.h, u0 + 2 * j + 2)
                                ic = s8 * 4 + m
                                vev.inc(v.tensor_tensor(
                                    hb[:, ic, tt * 512:(tt + 1) * 512],
                                    pvP[:, m % 2], gub[:, m, :], ALU.mult))
                    for _j in range(8):
                        hv_snap.append(v0 + _j + 1)
                    assert vev.n == v0 + 8

                for dq in range(2):
                    k3 = e * 2 + dq

                    @blk.gpsimd
                    def _(g, dq=dq, k3=k3):
                        if k3 >= 2:
                            ymm.wait(g, ym_snap[(k3 - 1) * 8 - 1] if
                                     len(ym_snap) >= (k3 - 1) * 8 else 0)
                        lw3[dq % 2].dma(g, w3s[:, dq % 2], w3src(dq))
                        w3_snap[k3] = lw3[dq % 2].n

                    for tk in range(8):
                        j3 = k3 * 8 + tk

                        @blk.tensor
                        def _(t, dq=dq, tk=tk, k3=k3, j3=j3):
                            t.wait_ge(lw3[dq % 2].h, w3_snap[k3])
                            vev.wait(t)
                            if j3 >= 2:
                                yev.wait(t, yv_snap[j3 - 2])
                            for ic in range(32):
                                li = t.matmul(pyP[:, tk % 2],
                                         hb[:, ic, tk * 128:(tk + 1) * 128],
                                         w3s[:, dq % 2, ic, :],
                                         start=(ic == 0), stop=(ic == 31))
                            ymm.inc(li)
                            ym_snap.append(ymm.n)

                        @blk.vector
                        def _(v, e=e, dq=dq, tk=tk):
                            ymm.wait(v)
                            dsl = ysb[:, tk, dq * 512:(dq + 1) * 512]
                            if e == 0:
                                yev.inc(v.tensor_copy(dsl, pyP[:, tk % 2]))
                            else:
                                yev.inc(v.scalar_tensor_tensor(
                                    dsl, pyP[:, tk % 2], msk[:, tk, e - 1:e],
                                    dsl, ALU.mult, ALU.add))
                            yv_snap.append(yev.n)

            for en in ("tensor", "vector", "scalar", "sync", "gpsimd"):
                @getattr(blk, en)
                def _(eg):
                    yev.wait(eg)

    def _exit_quarter(q, t0, pfx, ysb, do_ffn, glg):
        ex = dsem("xtx"); em = sem("xtm"); evv = sem("xtv")
        eo = dsem("xto")
        with (
            nc.sbuf_tensor(pfx + "_ext", [128, 2, NC, 128], F32) as ext,
            nc.sbuf_tensor(pfx + "_eos", [128, 2, D], F32) as eos,
            nc.psum_tensor(pfx + "_ep", [128, 2, D], F32) as epx,
        ):
            xs3, ms3, vs3, os3 = [], [], [], []
            for tk in range(8):
                ta = t0 + tk * 128
                bb, s0 = ta // S, ta % S

                @blk.sync
                def _(sy, tk=tk, ta=ta):
                    if tk >= 2:
                        em.wait(sy, ms3[tk - 2])
                    ex[tk % 2].dma(sy, ext[:, tk % 2], xr[:, :, ta:ta + 128])
                    xs3.append(ex[tk % 2].n)

                @blk.tensor
                def _(t, tk=tk):
                    t.wait_ge(ex[tk % 2].h, xs3[tk])
                    if tk >= 2:
                        evv.wait(t, vs3[tk - 2])
                    for c in range(NC):
                        li = t.matmul(epx[:, tk % 2, c * 128:(c + 1) * 128],
                                 ext[:, tk % 2][:, c, :], id32_sb[:],
                                 start=(c % 4 == 0), stop=(c % 4 == 3),
                                 skip_group_check=True)
                    em.inc(li)
                    ms3.append(em.n)

                @blk.vector
                def _(v, tk=tk):
                    v.wait_ge(em.h, ms3[tk])
                    if not do_ffn:
                        glg.wait(v)
                    if tk >= 2:
                        eo[tk % 2].wait(v, os3[tk - 2])
                    evv.inc(v.tensor_tensor(eos[:, tk % 2], epx[:, tk % 2],
                                            ysb[:, tk, :], ALU.add))
                    vs3.append(evv.n)

                @blk.scalar
                def _(s, tk=tk, bb=bb, s0=s0):
                    s.wait_ge(evv.h, vs3[tk])
                    eo[tk % 2].dma(s, out_t[bb, s0:s0 + 128, :], eos[:, tk % 2])
                    os3.append(eo[tk % 2].n)

            for en in ("scalar", "sync", "vector", "tensor", "gpsimd"):
                @getattr(blk, en)
                def _(e):
                    eo[0].wait(e)
                    eo[1].wait(e)

    if blocks >= 1:
        attn_phase(0)
    if blocks >= 2:
        p2_phase()
    if blocks >= 3:
        attn_phase(1)
    p4_phase(do_ffn=(blocks >= 4))

    es.close()
    return nc


_prog_cache = {}


def _get_prog(blocks=4):
    if blocks not in _prog_cache:
        _prog_cache[blocks] = build_program(blocks)
    return _prog_cache[blocks]


def kernel(**inputs):
    import ml_dtypes
    nc = _get_prog(4)
    x = np.ascontiguousarray(np.asarray(inputs["L0_attn_x"], dtype=np.float32))
    id32v = np.eye(128, dtype=np.float32)
    idbfv = np.eye(128, dtype=ml_dtypes.bfloat16)
    in_maps = []
    for core in range(NCORES):
        m = {"x_in": x[core * SPC:(core + 1) * SPC], "id32": id32v,
             "idbf": idbfv}
        for k, vv in inputs.items():
            if k != "L0_attn_x":
                a = np.ascontiguousarray(np.asarray(vv, dtype=np.float32))
                if a.ndim == 0:
                    a = a.reshape(1)
                m[k] = a
        in_maps.append(m)
    res = run_bass_kernel_spmd(nc, in_maps, core_ids=list(range(NCORES)))
    return np.concatenate([r["out"] for r in res.results], axis=0)


# version 05
# speedup vs baseline: 1.0117x; 1.0117x over previous
import numpy as np
from contextlib import ExitStack

import concourse.bass as bass
import concourse.mybir as mybir
from concourse.bass_utils import run_bass_kernel_spmd

F32 = mybir.dt.float32
BF16 = mybir.dt.bfloat16
AF = mybir.ActivationFunctionType
ALU = mybir.AluOpType
AX = mybir.AxisListType

NCORES = 8
B, S, D = 16, 2048, 1024
SPC = B // NCORES
T = SPC * S
NC = D // 128
NH = 16
INTER = 4096
NEXP = 8
EPS = 1e-5


class Sem:
    def __init__(self, nc, name):
        self.h = nc.alloc_semaphore(name)
        self.n = 0

    def inc(self, instr, k=1):
        instr.then_inc(self.h, k)
        self.n += k
        return instr

    def dma(self, eng, out, in_):
        return self.inc(eng.dma_start(out=out, in_=in_), 16)

    def wait(self, eng, n=None):
        eng.wait_ge(self.h, self.n if n is None else n)


def build_program(blocks=4):
    nc = bass.Bass("TRN2", target_bir_lowering=False, debug=False,
                   detect_race_conditions=False)

    x_in = nc.dram_tensor("x_in", [SPC, S, D], F32, kind="ExternalInput")
    out_t = nc.dram_tensor("out", [SPC, S, D], F32, kind="ExternalOutput")
    id32 = nc.dram_tensor("id32", [128, 128], F32, kind="ExternalInput")
    idbf = nc.dram_tensor("idbf", [128, 128], BF16, kind="ExternalInput")

    W = {}
    def win(name, shape):
        W[name] = nc.dram_tensor(name, list(shape), F32, kind="ExternalInput")
    for n in ("q", "k", "v", "o"):
        win(f"L0_attn_w_{n}", (D, D))
        win(f"L0_attn_s_{n}", (1,))
        win(f"L0_attn_b_{n}", (D,))
    for i, (a, b) in enumerate([(D, D), (D, 512), (512, 256), (256, 1)]):
        win(f"L0_ffn_cn_w{i+1}", (a, b))
        win(f"L0_ffn_cn_b{i+1}", (b,))
    for i, (a, b) in enumerate([(D, 512), (512, 256), (256, 1)]):
        win(f"L0_ffn_gc_w{i+1}", (a, b))
        win(f"L0_ffn_gc_b{i+1}", (b,))
    win("L0_ffn_adaptive_range", (1,))
    for n in ("Wq", "Wk", "Wv"):
        win(f"L1_attn_{n}", (NH, D, 64))
    win("L1_attn_Wp", (D, D))
    win("L1_attn_bp", (D,))
    win("L1_ffn_w1s", (D, INTER))
    win("L1_ffn_w2s", (D, INTER))
    win("L1_ffn_w3s", (INTER, D))
    win("L1_ffn_W1", (NEXP, D, INTER))
    win("L1_ffn_W2", (NEXP, D, INTER))
    win("L1_ffn_W3", (NEXP, INTER, D))
    win("L1_ffn_Wg", (D, NEXP))

    xr = nc.dram_tensor("xr", [128, NC, T], F32, kind="Internal")
    wt0 = nc.dram_tensor("wt0", [4, 128, NC, D], BF16, kind="Internal")

    es = ExitStack()
    sbuf = lambda name, shape, dt: es.enter_context(nc.sbuf_tensor(name, shape, dt))

    id32_sb = sbuf("id32_sb", [128, 128], F32)
    idbf_sb = sbuf("idbf_sb", [128, 128], BF16)
    ones1 = sbuf("ones1", [1, 128], F32)
    invD = sbuf("invD", [128, 1], BF16)
    sclrw = sbuf("sclrw", [1, 8], F32)
    scl = sbuf("scl", [128, 8], F32)
    bQ = sbuf("bQ", [128, NC], F32)
    bK = sbuf("bK", [128, NC], F32)
    bV = sbuf("bV", [128, NC], F32)
    bO = sbuf("bO", [128, NC], F32)
    bP = sbuf("bP", [128, NC], F32)
    bv_row = sbuf("bv_row", [1, D], F32)
    bp_zero = sbuf("bp_zero", [128, 1], F32)
    b1s = sbuf("b1s", [128, 8], F32)
    b2s = sbuf("b2s", [128, 4], F32)
    b3s = sbuf("b3s", [128, 2], F32)
    gb1 = sbuf("gb1", [128, 4], F32)
    gb2 = sbuf("gb2", [128, 2], F32)
    rowc = sbuf("rowc", [1, 8], F32)
    cgrow = sbuf("cgrow", [1, 2], F32)
    srow2 = sbuf("srow2", [1, 4], F32)

    blk = es.enter_context(nc.Block())

    _sems = {}
    def sem(name):
        if name not in _sems:
            _sems[name] = Sem(nc, name)
        return _sems[name]

    def dsem(name, k=2):
        return [sem(name + chr(97 + j)) for j in range(k)]

    pb = sem("pb")
    _nbar = [0]

    def barrier():
        _nbar[0] += 1
        tgt = 5 * _nbar[0]
        for en in ("tensor", "vector", "scalar", "sync", "gpsimd"):
            @getattr(blk, en)
            def _(e):
                e.drain().then_inc(pb.h, 1)
                e.wait_ge(pb.h, tgt)
        pb.n = tgt

    ld0 = sem("ld0"); g0 = sem("g0"); mm0 = sem("mm0"); ev0 = sem("ev0")

    @blk.sync
    def _(sy):
        es_nc = nc.allow_non_contiguous_dma("tiny one-time bias/scalar loads")
        es_nc.__enter__()
        ld0.dma(sy, id32_sb[:], id32[:])
        ld0.dma(sy, idbf_sb[:], idbf[:])
        for i, n in enumerate(("q", "k", "v", "o")):
            ld0.dma(sy, sclrw[0:1, i:i + 1], W[f"L0_attn_s_{n}"].ap()[None, :])
        ld0.dma(sy, sclrw[0:1, 4:5], W["L0_ffn_adaptive_range"].ap()[None, :])
        for dst, src in ((bQ, "L0_attn_b_q"), (bK, "L0_attn_b_k"),
                         (bV, "L0_attn_b_v"), (bO, "L0_attn_b_o"),
                         (bP, "L1_attn_bp")):
            ld0.dma(sy, dst[:], W[src].ap().rearrange("(c p) -> p c", p=128))
        ld0.dma(sy, bv_row[0:1, :], W["L0_attn_b_v"].ap()[None, :])
        ld0.dma(sy, b1s[:], W["L0_ffn_cn_b1"].ap().rearrange("(c p) -> p c", p=128))
        ld0.dma(sy, b2s[:], W["L0_ffn_cn_b2"].ap().rearrange("(c p) -> p c", p=128))
        ld0.dma(sy, b3s[:], W["L0_ffn_cn_b3"].ap().rearrange("(c p) -> p c", p=128))
        ld0.dma(sy, gb1[:], W["L0_ffn_gc_b1"].ap().rearrange("(c p) -> p c", p=128))
        ld0.dma(sy, gb2[:], W["L0_ffn_gc_b2"].ap().rearrange("(c p) -> p c", p=128))
        ld0.dma(sy, rowc[0:1, 0:1], W["L0_ffn_cn_b4"].ap()[None, :])
        ld0.dma(sy, rowc[0:1, 1:2], W["L0_ffn_gc_b3"].ap()[None, :])
        es_nc.__exit__(None, None, None)

    @blk.gpsimd
    def _(g):
        g.memset(ones1[:], 1.0)
        g.memset(bp_zero[:], 0.0)
        g.memset(rowc[0:1, 6:7], EPS)
        g0.inc(g.memset(invD[:], 1.0 / 1024.0))

    with nc.psum_tensor("p0scl", [128, 8], F32) as p0scl:
        @blk.tensor
        def _(t):
            ld0.wait(t)
            g0.wait(t)
            mm0.inc(t.matmul(p0scl[:, 0:5], ones1[:], sclrw[0:1, 0:5], start=True, stop=True))

        @blk.vector
        def _(v):
            mm0.wait(v)
            v.tensor_copy(scl[:, 0:5], p0scl[:, 0:5])
            v.tensor_scalar(rowc[0:1, 2:3], sclrw[0:1, 4:5], 0.3, None, ALU.mult)
            v.tensor_scalar(rowc[0:1, 3:4], rowc[0:1, 0:1], 0.5, None, ALU.mult)
            v.tensor_scalar(rowc[0:1, 4:5], rowc[0:1, 1:2], 0.5, None, ALU.mult)
            ev0.inc(v.tensor_scalar(rowc[0:1, 5:6], sclrw[0:1, 4:5], 0.7, None,
                                    ALU.mult))

    with (
        nc.sbuf_tensor("e_xt", [128, 2, D], F32) as e_xt,
        nc.sbuf_tensor("e_fm", [128, 2, D], F32) as e_fm,
        nc.psum_tensor("e_p", [128, 2, D], F32) as e_p,
    ):
        ldx = dsem("p0ldx"); mmx = sem("p0mmx"); evx = sem("p0evx"); stx = dsem("p0stx")
        ld_s, mm_s, ev_s, st_s = [], [], [], []
        for i in range(T // 128):
            bb, s0 = (i * 128) // S, (i * 128) % S

            @blk.sync
            def _(sy, i=i, bb=bb, s0=s0):
                if i >= 2:
                    mmx.wait(sy, mm_s[i - 2])
                ldx[i % 2].dma(sy, e_xt[:, i % 2], x_in[bb, s0:s0 + 128, :])
                ld_s.append(ldx[i % 2].n)

            @blk.tensor
            def _(t, i=i):
                if i == 0:
                    ev0.wait(t)
                t.wait_ge(ldx[i % 2].h, ld_s[i])
                if i >= 2:
                    evx.wait(t, ev_s[i - 2])
                for c in range(NC):
                    li = t.matmul(e_p[:, i % 2, c * 128:(c + 1) * 128],
                             e_xt[:, i % 2][:, c * 128:(c + 1) * 128], id32_sb[:],
                             start=(c % 4 == 0), stop=(c % 4 == 3),
                             skip_group_check=True)
                mmx.inc(li)
                mm_s.append(mmx.n)

            @blk.vector
            def _(v, i=i):
                v.wait_ge(mmx.h, mm_s[i])
                if i >= 2:
                    stx[i % 2].wait(v, st_s[i - 2])
                evx.inc(v.tensor_copy(e_fm[:, i % 2], e_p[:, i % 2]))
                ev_s.append(evx.n)

            @blk.scalar
            def _(s, i=i):
                s.wait_ge(evx.h, ev_s[i])
                stx[i % 2].dma(s, xr[:, :, i * 128:(i + 1) * 128],
                        e_fm[:, i % 2].rearrange("p (c t) -> p c t", c=NC))
                st_s.append(stx[i % 2].n)

        @blk.scalar
        def _(s):
            stx[0].wait(s)
            stx[1].wait(s)

    barrier()

    if blocks >= 1:
        with (
            nc.sbuf_tensor("w_tmp", [128, 2, NC, D], BF16) as w_tmp,
            nc.sbuf_tensor("w_tsb", [128, 2, D], BF16) as w_tsb,
            nc.psum_tensor("w_p", [128, 2, D], F32) as w_p,
        ):
            ldw = dsem("p0ldw"); mmw = sem("p0mmw"); evw = sem("p0evw"); stw = dsem("p0stw")
            lds, mms, evs, sts = [], [], [], []
            for w, nme in enumerate(("q", "k", "v", "o")):
                @blk.gpsimd
                def _(g, w=w, nme=nme):
                    if w >= 2:
                        mmw.wait(g, mms[(w - 1) * NC - 1])
                    ldw[w % 2].dma(g, w_tmp[:, w % 2],
                            W[f"L0_attn_w_{nme}"].ap().rearrange(
                                "(r p) h -> p r h", p=128))
                    lds.append(ldw[w % 2].n)

                for r in range(NC):
                    gi = w * NC + r

                    @blk.tensor
                    def _(t, w=w, r=r, gi=gi):
                        t.wait_ge(ldw[w % 2].h, lds[w])
                        if gi >= 2:
                            evw.wait(t, evs[gi - 2])
                        for c in range(NC):
                            li = t.matmul(w_p[:, gi % 2, c * 128:(c + 1) * 128],
                                     w_tmp[:, w % 2, r, c * 128:(c + 1) * 128],
                                     idbf_sb[:],
                                     start=(c % 4 == 0), stop=(c % 4 == 3),
                                     skip_group_check=True)
                        mmw.inc(li)
                        mms.append(mmw.n)

                    @blk.vector
                    def _(v, gi=gi):
                        v.wait_ge(mmw.h, mms[gi])
                        if gi >= 2:
                            stw[gi % 2].wait(v, sts[gi - 2])
                        evw.inc(v.tensor_copy(w_tsb[:, gi % 2], w_p[:, gi % 2]))
                        evs.append(evw.n)

                    @blk.scalar
                    def _(s, w=w, r=r, gi=gi):
                        s.wait_ge(evw.h, evs[gi])
                        stw[gi % 2].dma(s, wt0[w, :, :, r * 128:(r + 1) * 128],
                                w_tsb[:, gi % 2].rearrange("p (c o) -> p c o", c=NC))
                        sts.append(stw[gi % 2].n)

            @blk.scalar
            def _(s):
                stw[0].wait(s)
                stw[1].wait(s)

    barrier()

    _lnid = [0]

    def ln_pass(t0, t1, zb, zb_off_fn, store_rows=None, extra=None):
        _lnid[0] += 1
        pfx = "ln"
        px = dsem(pfx + "x"); pc = sem(pfx + "c"); pm = sem(pfx + "m")
        pq = sem(pfx + "q"); pv = sem(pfx + "v"); pr = sem(pfx + "r")
        pm2 = sem(pfx + "b"); pz = sem(pfx + "z")
        n = (t1 - t0) // 256
        ufx = "ln%d" % _lnid[0]
        with (
            nc.sbuf_tensor(ufx + "_xt", [128, 2, NC, 256], F32) as xt,
            nc.sbuf_tensor(ufx + "_xb", [128, NC, 256], BF16) as xb,
            nc.sbuf_tensor(ufx + "_xq", [128, NC, 256], BF16) as xq,
            nc.sbuf_tensor(ufx + "_zt", [128, 256], F32) as zt,
            nc.sbuf_tensor(ufx + "_rw", [1, 1024], F32) as rw,
            nc.psum_tensor(ufx + "_ps", [1, 2, 512], F32) as ps,
            nc.psum_tensor(ufx + "_pb", [128, 2, 512], F32) as pbx,
        ):
            px0 = [px[0].n, px[1].n]
            pc0, pm0, pq0, pv0, pr0, pb0, pz0 = (pc.n, pm.n, pq.n, pv.n,
                                                 pr.n, pm2.n, pz.n)
            zs = []
            for i in range(n):
                ta = t0 + i * 256

                @blk.sync
                def _(sy, i=i, ta=ta):
                    if i >= 2:
                        sy.wait_ge(pz.h, pz0 + i - 1)
                    px[i % 2].dma(sy, xt[:, i % 2], xr[:, :, ta:ta + 256])

                @blk.scalar
                def _(s, i=i):
                    s.wait_ge(px[i % 2].h, px0[i % 2] + 16 * (i // 2 + 1))
                    if i >= 1:
                        s.wait_ge(pm.h, pm0 + i)
                        s.wait_ge(pm2.h, pb0 + i)
                        if store_rows is not None:
                            s.wait_ge(pz.h, pz0 + i)
                    s.activation(xb[:], xt[:, i % 2], AF.Identity)
                    pc.inc(s.activation(xq[:], xt[:, i % 2], AF.Square))
                    s.wait_ge(pm.h, pm0 + i + 1)
                    s.activation(rw[0:1, 0:256], ps[0:1, i % 2, 0:256], AF.Identity)
                    pq.inc(s.activation(rw[0:1, 512:768], ps[0:1, i % 2, 0:256],
                                        AF.Square))
                    s.wait_ge(pv.h, pv0 + i + 1)
                    pr.inc(s.activation(rw[0:1, 768:1024], rw[0:1, 768:1024],
                                        AF.Ln, bias=rowc[0:1, 6:7]))
                    s.wait_ge(pr.h, pr.n)
                    pr.inc(s.activation(rw[0:1, 256:512], rw[0:1, 768:1024],
                                        AF.Exp, scale=-0.5))

                @blk.tensor
                def _(t, i=i):
                    t.wait_ge(pc.h, pc0 + i + 1)
                    if i >= 2:
                        t.wait_ge(pv.h, pv0 + i - 1)
                    for c in range(NC):
                        t.matmul(ps[0:1, i % 2, 0:256], invD[:], xb[:, c, :],
                                 start=(c == 0), stop=(c == NC - 1),
                                 skip_group_check=True)
                    for c in range(NC):
                        li = t.matmul(ps[0:1, i % 2, 256:512], invD[:], xq[:, c, :],
                                 start=False, stop=(c == NC - 1),
                                 skip_group_check=True)
                    pm.inc(li)
                    t.wait_ge(pr.h, pr0 + 2 * i + 2)
                    if i >= 2:
                        t.wait_ge(pz.h, pz0 + i - 1)
                    pm2.inc(t.matmul(pbx[:, i % 2], ones1[:], rw[0:1, 0:512],
                             start=True, stop=True, skip_group_check=True))

                @blk.vector
                def _(v, i=i, ta=ta):
                    v.wait_ge(pq.h, pq0 + i + 1)
                    pv.inc(v.tensor_tensor(rw[0:1, 768:1024],
                                           ps[0:1, i % 2, 256:512],
                                           rw[0:1, 512:768], ALU.subtract))
                    v.wait_ge(pm2.h, pb0 + i + 1)
                    zo = zb_off_fn(i, ta)
                    for c in range(NC):
                        v.tensor_tensor(zt[:], xt[:, i % 2][:, c, :],
                                        pbx[:, i % 2][:, 0:256], ALU.subtract)
                        v.tensor_tensor(zb[:, c, zo:zo + 256], zt[:],
                                        pbx[:, i % 2][:, 256:512], ALU.mult)
                    if store_rows is not None:
                        mrow, rrow = store_rows
                        v.tensor_copy(mrow[0:1, ta - t0:ta - t0 + 256],
                                      rw[0:1, 0:256])
                        v.tensor_copy(rrow[0:1, ta - t0:ta - t0 + 256],
                                      rw[0:1, 256:512])
                    pz.inc(v.drain())
                zs.append(pz0 + i + 1)

                if extra is not None:
                    extra(i, ta, zs[i], pz)

            for en in ("vector", "tensor", "scalar", "sync", "gpsimd"):
                @getattr(blk, en)
                def _(e):
                    pz.wait(e)
        return pz

    def attn_phase(layer):
        Lp = "a%d" % layer
        with (
            nc.sbuf_tensor(Lp + "_Q", [128, NC, S], BF16) as Qs,
            nc.sbuf_tensor(Lp + "_K", [128, NC, S], BF16) as Ks,
            nc.sbuf_tensor(Lp + "_V", [128, 16, NH, 65], BF16) as Vs,
            nc.sbuf_tensor(Lp + "_bvsb", [128, D], F32) as bv_sb,
            nc.sbuf_tensor(Lp + "_wo", [128, NC, D], BF16) as wo_sb,
        ):
            lwo = sem("aplwo%d" % layer)
            if layer == 0:
                @blk.sync
                def _(sy):
                    lwo.dma(sy, wo_sb[:], wt0[3])
            else:
                @blk.gpsimd
                def _(g):
                    wpv = W["L1_attn_Wp"].ap().rearrange(
                        "(e c a) o -> e c a o", e=64, c=8, a=2)
                    for aa in range(2):
                        lwo.dma(g, wo_sb[aa * 64:(aa + 1) * 64, :, :],
                                wpv[:, :, aa, :])
            lwo_snap = lwo.n

            if layer == 0:
                pvb = sem("apvb")
                with nc.psum_tensor(Lp + "_pvb", [128, D], F32) as p_vb:
                    @blk.tensor
                    def _(t):
                        ev0.wait(t)
                        for half in range(2):
                            li = t.matmul(p_vb[:, half * 512:(half + 1) * 512],
                                     ones1[:],
                                     bv_row[0:1, half * 512:(half + 1) * 512],
                                     start=True, stop=True,
                                     skip_group_check=True)
                        pvb.inc(li)

                    @blk.vector
                    def _(v):
                        pvb.wait(v)
                        pvb.inc(v.tensor_copy(bv_sb[:], p_vb[:]))

            gv = sem("apgv")

            @blk.gpsimd
            def _(g):
                gv.inc(g.memset(Vs[:, :, :, 64:65], 1.0))

            if layer == 0:
                @blk.tensor
                def _(t):
                    pvb.wait(t)

            for seq in range(SPC):
                t0 = seq * S
                with nc.sbuf_tensor(Lp + "_zb%d" % seq, [128, NC, S], BF16) as zb:
                    pz = ln_pass(t0, t0 + S, zb, lambda i, ta, t0=t0: ta - t0)
                    _proj(layer, seq, zb, Qs, Ks, Vs, bv_sb, pz, gv)
                _attn_core(layer, seq, t0, Qs, Ks, Vs, wo_sb, lwo, lwo_snap)
        barrier()

    def _proj(layer, seq, zb, Qs, Ks, Vs, bv_sb, pz, gv):
        Lp = "a%dp%d" % (layer, seq)
        lw = dsem("prlw%d" % layer); pmm = sem("prmm"); pev = sem("prev")
        with (
            nc.sbuf_tensor(Lp + "_w", [128, 2, NC, D], BF16) as w_sb,
            nc.psum_tensor(Lp + "_pp", [128, 2, 512], F32) as pp,
        ):
            lws = []
            wdone = []
            evl = []
            slot = [0]

            for wi in range(3):
                eng = "sync" if layer == 0 else "gpsimd"

                @getattr(blk, eng)
                def _(e, wi=wi):
                    if wi >= 2:
                        pmm.wait(e, wdone[wi - 2])
                    if layer == 0:
                        lw[wi % 2].dma(e, w_sb[:, wi % 2], wt0[wi])
                    else:
                        nm = ("Wq", "Wk", "Wv")[wi]
                        wv4 = W[f"L1_attn_{nm}"].ap().rearrange(
                            "h (c p) e -> c p h e", p=128)
                        for c in range(NC):
                            lw[wi % 2].dma(
                                e,
                                w_sb[:, wi % 2, c, :].rearrange(
                                    "p (h e) -> p h e", e=64),
                                wv4[c])
                    lws.append(lw[wi % 2].n)

                if wi < 2:
                    dst = (Qs, Ks)[wi]
                    for ts in range(S // 512):
                        for m in range(NC):
                            j = slot[0]

                            @blk.tensor
                            def _(t, wi=wi, ts=ts, m=m, j=j):
                                t.wait_ge(lw[wi % 2].h, lws[wi])
                                pz.wait(t)
                                if j >= 2:
                                    pev.wait(t, evl[j - 2])
                                for c in range(NC):
                                    li = t.matmul(pp[:, j % 2],
                                             w_sb[:, wi % 2][:, c, m * 128:(m + 1) * 128],
                                             zb[:, c, ts * 512:(ts + 1) * 512],
                                             start=(c == 0), stop=(c == NC - 1))
                                pmm.inc(li)

                            @blk.vector
                            def _(v, wi=wi, ts=ts, m=m, j=j):
                                pmm.wait(v)
                                dd = dst[:, m, ts * 512:(ts + 1) * 512]
                                if layer == 0:
                                    pev.inc(v.tensor_scalar(
                                        dd, pp[:, j % 2], scl[:, wi:wi + 1],
                                        (bQ, bK)[wi][:, m:m + 1],
                                        ALU.mult, ALU.add))
                                else:
                                    pev.inc(v.tensor_copy(dd, pp[:, j % 2]))
                                evl.append(pev.n)

                            slot[0] += 1
                    wdone.append(pmm.n)
                else:
                    for tk in range(S // 128):
                        for vc in range(2):
                            j = slot[0]

                            @blk.tensor
                            def _(t, tk=tk, vc=vc, j=j):
                                t.wait_ge(lw[0].h, lws[2])
                                pz.wait(t)
                                gv.wait(t)
                                if j >= 2:
                                    pev.wait(t, evl[j - 2])
                                for c in range(NC):
                                    li = t.matmul(pp[:, j % 2],
                                             zb[:, c, tk * 128:(tk + 1) * 128],
                                             w_sb[:, 0, c, vc * 512:(vc + 1) * 512],
                                             start=(c == 0), stop=(c == NC - 1))
                                pmm.inc(li)

                            @blk.vector
                            def _(v, layer=layer, seq=seq, tk=tk, vc=vc, j=j):
                                pmm.wait(v)
                                dsl = Vs[:, tk, vc * 8:(vc + 1) * 8, 0:64]
                                src = pp[:, j % 2].rearrange("p (h e) -> p h e", e=64)
                                if layer == 0:
                                    bvv = bv_sb[:, vc * 512:(vc + 1) * 512] \
                                        .rearrange("p (h e) -> p h e", e=64)
                                    v.scalar_tensor_tensor(dsl, src, scl[:, 2:3],
                                                           bvv, ALU.mult, ALU.add)
                                else:
                                    v.tensor_copy(dsl, src)
                                pev.inc(v.drain())
                                evl.append(pev.n)

                            slot[0] += 1
                    wdone.append(pmm.n)

            for en in ("tensor", "vector", "scalar", "sync", "gpsimd"):
                @getattr(blk, en)
                def _(e):
                    pev.wait(e)

    def _attn_core(layer, seq, t0, Qs, Ks, Vs, wo_sb, lwo, lwo_snap):
        Lp = "a%dc%d" % (layer, seq)
        amm = sem("acmm"); aex = sem("acex"); acp = sem("accp")
        arw = sem("acrw"); abc = sem("acbc"); aev = sem("acev")
        aom = sem("acom"); alr = dsem("aclr"); aoe = sem("acoe")
        axv = sem("acxv"); axs = dsem("acxs")
        with (
            nc.sbuf_tensor(Lp + "_Pp", [128, 2, 1024], BF16) as Pp,
            nc.sbuf_tensor(Lp + "_ctx", [128, NC, 512], BF16) as ctx,
            nc.sbuf_tensor(Lp + "_rin", [64, 2, 512], F32) as rin,
            nc.sbuf_tensor(Lp + "_srow", [1, 1024], F32) as srow,
            nc.sbuf_tensor(Lp + "_xrt", [128, 2, NC, 512], F32) as xrt,
            nc.sbuf_tensor(Lp + "_otmp", [128, 2, 512], F32) as otmp,
            nc.psum_tensor(Lp + "_st", [128, 2, 1024], F32) as stp,
            nc.psum_tensor(Lp + "_cx", [65, 2, 512], F32) as cxp,
            nc.psum_tensor(Lp + "_rb", [64, 512], F32) as rbp,
            nc.psum_tensor(Lp + "_po", [128, 512], F32) as pop,
        ):
            rw_snap = {}; bc_snap = {}; rcopy = {}; ev_done = {}; bc_plan = {}
            rcopy_plan = {}; rw_plan = {}
            om_snap = []; oe_snap = []; xv_snap = []; st_snap = []; lr_snap = []
            pend_bc = [None]
            for qb in range(4):
                @blk.sync
                def _(sy, qb=qb):
                    if qb >= 2:
                        axs[qb % 2].wait(sy, st_snap[qb - 2])
                    alr[qb % 2].dma(sy, xrt[:, qb % 2],
                            xr[:, :, t0 + qb * 512: t0 + (qb + 1) * 512])
                    lr_snap.append(alr[qb % 2].n)

                qb_scalar = []
                for h in range(NH):
                    hp, ho = h // 2, (h % 2) * 64
                    gidx = qb * NH + h
                    amm0, aex0, acp0 = amm.n, aex.n, acp.n

                    @blk.tensor
                    def _(t, qb=qb, h=h, hp=hp, ho=ho, gidx=gidx,
                          aex0=aex0, acp0=acp0):
                        for g in range(8):
                            if gidx * 8 + g >= 2:
                                t.wait_ge(aex.h, aex0 + g - 1 if g >= 2
                                          else ex_tail[gidx - 1] - (1 - g))
                            for kt in (2 * g, 2 * g + 1):
                                li = t.matmul(
                                    stp[:, g % 2, (kt % 2) * 512:(kt % 2 + 1) * 512],
                                    Ks[ho:ho + 64, hp, kt * 128:(kt + 1) * 128],
                                    Qs[ho:ho + 64, hp, qb * 512:(qb + 1) * 512],
                                    start=True, stop=True, skip_group_check=True)
                            amm.inc(li)
                            if g >= 1:
                                t.wait_ge(aex.h, aex0 + g)
                                if g == 1 and gidx >= 2:
                                    aev.wait(t, ev_done[gidx - 2])
                                for kt in (2 * g - 2, 2 * g - 1):
                                    li = t.matmul(cxp[:, gidx % 2], Vs[:, kt, h, :],
                                             Pp[:, (g - 1) % 2, (kt % 2) * 512:(kt % 2 + 1) * 512],
                                             start=(kt == 0), stop=False,
                                             skip_group_check=True)
                                acp.inc(li)
                        t.wait_ge(aex.h, aex0 + 8)
                        for kt in (14, 15):
                            li = t.matmul(cxp[:, gidx % 2], Vs[:, kt, h, :],
                                     Pp[:, 1, (kt % 2) * 512:(kt % 2 + 1) * 512],
                                     start=False, stop=(kt == 15),
                                     skip_group_check=True)
                        acp.inc(li)
                        if pend_bc[0] is not None:
                            pend_bc[0](t)
                            pend_bc[0] = None
                    assert amm.n == amm0 + 8 and acp.n == acp0 + 8

                    def sc_emit(s, gidx=gidx, amm0=amm0, aex0=aex0, acp0=acp0):
                        for g in range(8):
                            s.wait_ge(amm.h, amm0 + g + 1)
                            if gidx * 8 + g >= 2:
                                s.wait_ge(acp.h, acp0 + g - 1 if g >= 2
                                          else cp_tail[gidx - 1] - (1 - g))
                            s.activation(Pp[:, g % 2], stp[:, g % 2],
                                         AF.Exp, scale=0.125).then_inc(aex.h, 1)
                        s.wait_ge(acp.h, acp0 + 8)
                        if gidx >= 1:
                            s.wait_ge(abc.h, bc_plan[gidx - 1])
                        s.activation(srow[0:1, 0:512], cxp[64:65, gidx % 2],
                                     AF.Ln).then_inc(arw.h, 1)
                        s.wait_ge(arw.h, rw_plan[gidx] - 1)
                        s.activation(srow[0:1, 512:1024], srow[0:1, 0:512],
                                     AF.Exp, scale=-1.0).then_inc(arw.h, 1)
                    qb_scalar.append(sc_emit)
                    aex.n += 8
                    rw_plan[gidx] = arw.n + 2
                    arw.n += 2
                    rw_snap[gidx] = rw_plan[gidx]
                    ex_tail = getattr(_attn_core, "_ex_tail", {})
                    ex_tail[gidx] = aex0 + 8
                    _attn_core._ex_tail = ex_tail
                    cp_tail = getattr(_attn_core, "_cp_tail", {})
                    cp_tail[gidx] = acp0 + 8
                    _attn_core._cp_tail = cp_tail

                    def mk_bc(gidx=gidx):
                        def emit(t):
                            t.wait_ge(arw.h, rw_snap[gidx])
                            if gidx >= 1:
                                t.wait_ge(aev.h, rcopy_plan[gidx - 1])
                            t.matmul(rbp[:], ones1[0:1, 0:64],
                                     srow[0:1, 512:1024], start=True, stop=True,
                                     skip_group_check=True).then_inc(abc.h, 1)
                        return emit
                    pend_bc[0] = mk_bc()
                    bc_plan[gidx] = abc.n + 1
                    abc.n += 1

                    @blk.vector
                    def _(v, qb=qb, hp=hp, ho=ho, gidx=gidx):
                        v.wait_ge(abc.h, bc_plan[gidx])
                        if qb >= 1 and gidx % NH == 0:
                            aom.wait(v, om_snap[qb - 1])
                        v.tensor_copy(rin[:, gidx % 2], rbp[:])
                        aev.inc(v.drain())
                        rcopy_plan[gidx] = aev.n
                        v.wait_ge(aev.h, aev.n)
                        aev.inc(v.tensor_tensor(ctx[ho:ho + 64, hp, :],
                                                cxp[0:64, gidx % 2],
                                                rin[:, gidx % 2], ALU.mult))
                        ev_done[gidx] = aev.n

                @blk.scalar
                def _(s, qb_scalar=qb_scalar):
                    for fn in qb_scalar:
                        fn(s)

                @blk.tensor
                def _(t):
                    if pend_bc[0] is not None:
                        pend_bc[0](t)
                        pend_bc[0] = None
                for m in range(NC):
                    j = qb * NC + m

                    @blk.tensor
                    def _(t, qb=qb, m=m, j=j):
                        t.wait_ge(lwo.h, lwo_snap)
                        aev.wait(t)
                        if j >= 1:
                            aoe.wait(t, oe_snap[j - 1])
                        for c in range(NC):
                            li = t.matmul(pop[:],
                                     wo_sb[:, c, m * 128:(m + 1) * 128],
                                     ctx[:, c, :],
                                     start=(c == 0), stop=(c == NC - 1))
                        aom.inc(li)
                        if m == NC - 1:
                            om_snap.append(aom.n)

                    @blk.vector
                    def _(v, qb=qb, m=m, j=j):
                        aom.wait(v)
                        if j >= 2:
                            axv.wait(v, xv_snap[j - 2])
                        if layer == 0:
                            bias, scale = bO[:, m:m + 1], scl[:, 3:4]
                        else:
                            bias, scale = bP[:, m:m + 1], 1.0
                        aoe.inc(v.tensor_scalar(otmp[:, j % 2], pop[:],
                                                scale, bias, ALU.mult, ALU.add))
                        oe_snap.append(aoe.n)
                        v.wait_ge(alr[qb % 2].h, lr_snap[qb])
                        axv.inc(v.tensor_tensor(xrt[:, qb % 2, m, :],
                                                xrt[:, qb % 2, m, :],
                                                otmp[:, j % 2], ALU.add))
                        xv_snap.append(axv.n)

                @blk.scalar
                def _(s, qb=qb):
                    axv.wait(s)
                    axs[qb % 2].dma(s, xr[:, :, t0 + qb * 512: t0 + (qb + 1) * 512],
                            xrt[:, qb % 2])
                    st_snap.append(axs[qb % 2].n)

            for en in ("scalar", "vector", "tensor", "sync"):
                @getattr(blk, en)
                def _(e):
                    axs[0].wait(e)
                    axs[1].wait(e)

    def p2_phase():
        with (
            nc.sbuf_tensor("p2_mrow", [1, T], F32) as mrow,
            nc.sbuf_tensor("p2_rrow", [1, T], F32) as rrow,
            nc.sbuf_tensor("p2_zbt", [128, NC, 512], BF16) as zbt,
            nc.sbuf_tensor("p2_gacc", [128, NC, 2], F32) as gacc,
            nc.sbuf_tensor("p2_gtmp", [128, NC], F32) as gtmp,
            nc.sbuf_tensor("p2_cn1", [128, NC, D], BF16) as cn1,
            nc.sbuf_tensor("p2_cn2", [128, NC, 512], BF16) as cn2,
            nc.sbuf_tensor("p2_cn3", [128, 4, 256], BF16) as cn3,
            nc.sbuf_tensor("p2_cn4", [128, 2, 1], BF16) as cn4,
            nc.sbuf_tensor("p2_gc1", [128, NC, 512], BF16) as gc1,
            nc.sbuf_tensor("p2_gc2", [128, 4, 256], BF16) as gc2,
            nc.sbuf_tensor("p2_gc3", [128, 2, 1], BF16) as gc3,
        ):
            lw2 = sem("p2lw"); gz = sem("p2gz"); gv2 = sem("p2gv")

            @blk.gpsimd
            def _(g):
                gz.inc(g.memset(gacc[:], 0.0))
                with nc.allow_non_contiguous_dma("tiny [.,1] weight columns"):
                    for nm, dst in (("cn_w1", cn1), ("cn_w2", cn2),
                                    ("cn_w3", cn3), ("cn_w4", cn4),
                                    ("gc_w1", gc1), ("gc_w2", gc2),
                                    ("gc_w3", gc3)):
                        lw2.dma(g, dst[:], W["L0_ffn_" + nm].ap().rearrange(
                            "(c p) o -> p c o", p=128))

            def gc_extra(i, ta, zsnap, pz):
                @blk.vector
                def _(v, i=i, ta=ta, zsnap=zsnap):
                    gz.wait(v)
                    sq = ta // S
                    off = (i % 2) * 256
                    v.tensor_reduce(gtmp[:], zbt[:, :, off:off + 256],
                                    AX.X, ALU.add)
                    gv2.inc(v.tensor_tensor(gacc[:, :, sq], gacc[:, :, sq],
                                            gtmp[:], ALU.add))

            ln_pass(0, T, zbt, lambda i, ta: (i % 2) * 256,
                    store_rows=(mrow, rrow), extra=gc_extra)

            gmm = sem("p2gmm"); gev = sem("p2gev")
            with (
                nc.sbuf_tensor("p2_gcb", [128, NC, 2], BF16) as gcb,
                nc.sbuf_tensor("p2_g1", [128, 4, 2], BF16) as g1b,
                nc.sbuf_tensor("p2_g2", [128, 2, 2], BF16) as g2b,
                nc.psum_tensor("p2_pg", [128, 8], F32) as pg,
                nc.psum_tensor("p2_pg2", [128, 4], F32) as pg2,
                nc.psum_tensor("p2_pg3", [1, 2], F32) as pg3,
            ):
                @blk.vector
                def _(v):
                    gv2.wait(v)
                    gev.inc(v.tensor_scalar(gcb[:], gacc[:], 1.0 / S, None,
                                            ALU.mult))

                @blk.tensor
                def _(t):
                    gev.wait(t)
                    lw2.wait(t)
                    for m in range(4):
                        for c in range(NC):
                            li = t.matmul(pg[:, m * 2:(m + 1) * 2],
                                     gc1[:, c, m * 128:(m + 1) * 128],
                                     gcb[:, c, :],
                                     start=(c == 0 and m == 0),
                                     stop=(c == NC - 1 and m == 3),
                                     skip_group_check=True)
                    gmm.inc(li)

                @blk.scalar
                def _(s):
                    gmm.wait(s)
                    for m in range(4):
                        s.activation(g1b[:, m, :], pg[:, m * 2:(m + 1) * 2],
                                     AF.Gelu, bias=gb1[:, m:m + 1])
                    gev.inc(s.drain())

                @blk.tensor
                def _(t):
                    gev.wait(t)
                    for m in range(2):
                        for c in range(4):
                            li = t.matmul(pg2[:, m * 2:(m + 1) * 2],
                                     gc2[:, c, m * 128:(m + 1) * 128],
                                     g1b[:, c, :],
                                     start=(c == 0 and m == 0),
                                     stop=(c == 3 and m == 1),
                                     skip_group_check=True)
                    gmm.inc(li)

                @blk.scalar
                def _(s):
                    gmm.wait(s)
                    for m in range(2):
                        s.activation(g2b[:, m, :], pg2[:, m * 2:(m + 1) * 2],
                                     AF.Gelu, bias=gb2[:, m:m + 1])
                    gev.inc(s.drain())

                @blk.tensor
                def _(t):
                    gev.wait(t)
                    for c in range(2):
                        li = t.matmul(pg3[:], gc3[:, c, :], g2b[:, c, :],
                                 start=(c == 0), stop=(c == 1),
                                 skip_group_check=True)
                    gmm.inc(li)

                @blk.scalar
                def _(s):
                    gmm.wait(s)
                    gev.inc(s.activation(srow2[0:1, 0:2], pg3[0:1, :], AF.Tanh,
                                         scale=0.5, bias=rowc[0:1, 4:5]))

                @blk.vector
                def _(v):
                    gev.wait(v)
                    gev.inc(v.tensor_scalar(cgrow[0:1, :], srow2[0:1, 0:2],
                                            rowc[0:1, 5:6], 0.1,
                                            ALU.mult, ALU.add))

            cx = dsem("p2cx"); cb = sem("p2cb"); cmm = sem("p2cmm")
            cge = sem("p2cge"); ctf = sem("p2ctf"); ccr = sem("p2ccr")
            cbc = sem("p2cbc"); cup = sem("p2cup"); cst = dsem("p2cst")
            cca = sem("p2cca")
            SGN = [1.0, 1.0, -1.0, -1.0, -1.0, -1.0, 1.0, 1.0]
            PRT = [4, 5, 6, 7, 0, 1, 2, 3]
            es2 = ExitStack()
            xrt = es2.enter_context(nc.sbuf_tensor("p2_xrt", [128, 2, NC, 512], F32))
            oA = es2.enter_context(nc.sbuf_tensor("p2_oA", [128, NC, 512], F32))
            oB = es2.enter_context(nc.sbuf_tensor("p2_oB", [128, NC, 512], F32))
            ob = es2.enter_context(nc.sbuf_tensor("p2_ob", [128, NC, 512], BF16))
            u1b = es2.enter_context(nc.sbuf_tensor("p2_u1", [128, NC, 512], BF16))
            u2b = es2.enter_context(nc.sbuf_tensor("p2_u2", [128, 4, 512], BF16))
            u3b = es2.enter_context(nc.sbuf_tensor("p2_u3", [128, 2, 512], BF16))
            csb = es2.enter_context(nc.sbuf_tensor("p2_csb", [128, 512], F32))
            ztm = es2.enter_context(nc.sbuf_tensor("p2_ztm", [128, 512], F32))
            trow = es2.enter_context(nc.sbuf_tensor("p2_trow", [1, 512], F32))
            crow = es2.enter_context(nc.sbuf_tensor("p2_crow", [1, 512], F32))
            pb2 = es2.enter_context(nc.psum_tensor("p2_pb2", [128, 1024], F32))
            pu = es2.enter_context(nc.psum_tensor("p2_pu", [128, 2, 512], F32))
            p4 = es2.enter_context(nc.psum_tensor("p2_p4", [1, 512], F32))
            pcr = es2.enter_context(nc.psum_tensor("p2_pcr", [128, 512], F32))
            if True:
                xs2, st2 = [], []

                @blk.tensor
                def _(t):
                    gev.wait(t)
                @blk.sync
                def _(sy):
                    gev.wait(sy)

                def dense_layer(win_sb, bsb, uout, nm_, nk_, cycsem):
                    c0, g0 = cmm.n, cge.n

                    @blk.tensor
                    def _(t):
                        cge.wait(t, g0)
                        for m in range(nm_):
                            if m >= 2:
                                t.wait_ge(cge.h, g0 + m - 1)
                            for c in range(nk_):
                                li = t.matmul(pu[:, m % 2],
                                         win_sb[:, c, m * 128:(m + 1) * 128],
                                         cycsem[0][:, c, :],
                                         start=(c == 0), stop=(c == nk_ - 1))
                            cmm.inc(li)

                    @blk.scalar
                    def _(s):
                        for m in range(nm_):
                            s.wait_ge(cmm.h, c0 + m + 1)
                            cge.inc(s.activation(uout[:, m, :], pu[:, m % 2],
                                                 AF.Gelu, bias=bsb[:, m:m + 1]))

                for ti in range(T // 512):
                    ta = ti * 512
                    sq = ta // S

                    @blk.sync
                    def _(sy, ti=ti, ta=ta):
                        if ti >= 2:
                            cst[ti % 2].wait(sy, st2[ti - 2])
                        cx[ti % 2].dma(sy, xrt[:, ti % 2], xr[:, :, ta:ta + 512])
                        xs2.append(cx[ti % 2].n)

                    @blk.tensor
                    def _(t, ti=ti, ta=ta):
                        t.wait_ge(cx[ti % 2].h, xs2[ti])
                        cca.wait(t)
                        t.matmul(pb2[:, 0:512], ones1[:], mrow[0:1, ta:ta + 512],
                                 start=True, stop=True, skip_group_check=True)
                        cb.inc(t.matmul(pb2[:, 512:1024], ones1[:],
                                 rrow[0:1, ta:ta + 512],
                                 start=True, stop=True, skip_group_check=True))

                    @blk.vector
                    def _(v, ti=ti):
                        cb.wait(v)
                        for c in range(NC):
                            v.tensor_tensor(ztm[:], xrt[:, ti % 2, c, :],
                                            pb2[:, 0:512], ALU.subtract)
                            v.tensor_tensor(oA[:, c, :], ztm[:],
                                            pb2[:, 512:1024], ALU.mult)
                        cup.inc(v.drain())

                    @blk.scalar
                    def _(s, ti=ti):
                        cup.wait(s)
                        for c in range(NC):
                            s.activation(ob[:, c, :], oA[:, c, :], AF.Identity)
                        cge.inc(s.drain())

                    for cyc in range(3):
                        src, dst = (oA, oB) if cyc % 2 == 0 else (oB, oA)
                        dense_layer(cn1, b1s, u1b, NC, NC, [ob])
                        dense_layer(cn2, b2s, u2b, 4, NC, [u1b])
                        dense_layer(cn3, b3s, u3b, 2, 4, [u2b])

                        @blk.tensor
                        def _(t):
                            cge.wait(t)
                            for c in range(2):
                                li = t.matmul(p4[:], cn4[:, c, :], u3b[:, c, :],
                                         start=(c == 0), stop=(c == 1),
                                         skip_group_check=True)
                            cmm.inc(li)

                        @blk.scalar
                        def _(s):
                            cmm.wait(s)
                            ctf.inc(s.activation(trow[0:1, :], p4[0:1, :],
                                                 AF.Tanh, scale=0.5,
                                                 bias=rowc[0:1, 3:4]))

                        @blk.vector
                        def _(v, sq=sq):
                            ctf.wait(v)
                            ccr.inc(v.tensor_scalar(crow[0:1, :], trow[0:1, :],
                                                    rowc[0:1, 2:3],
                                                    cgrow[0:1, sq:sq + 1],
                                                    ALU.mult, ALU.add))

                        @blk.tensor
                        def _(t):
                            ccr.wait(t)
                            cbc.inc(t.matmul(pcr[:], ones1[:], crow[0:1, :],
                                     start=True, stop=True,
                                     skip_group_check=True))

                        @blk.vector
                        def _(v, src=src, dst=dst):
                            cbc.wait(v)
                            v.tensor_copy(csb[:], pcr[:])
                            cup.inc(v.drain())
                            v.wait_ge(cup.h, cup.n)
                            for c in range(NC):
                                v.tensor_tensor(ztm[:], csb[:],
                                                src[:, PRT[c], :], ALU.mult)
                                v.scalar_tensor_tensor(dst[:, c, :], ztm[:],
                                                       SGN[c], src[:, c, :],
                                                       ALU.mult, ALU.add)
                            cup.inc(v.drain())

                        if cyc < 2:
                            @blk.scalar
                            def _(s, dst=dst):
                                cup.wait(s)
                                for c in range(NC):
                                    s.activation(ob[:, c, :], dst[:, c, :],
                                                 AF.Identity)
                                cge.inc(s.drain())

                    @blk.vector
                    def _(v, ti=ti):
                        cup.wait(v)
                        for c in range(NC):
                            v.tensor_tensor(xrt[:, ti % 2, c, :],
                                            xrt[:, ti % 2, c, :],
                                            oB[:, c, :], ALU.add)
                        cca.inc(v.drain())

                    @blk.scalar
                    def _(s, ti=ti, ta=ta):
                        cca.wait(s)
                        cst[ti % 2].dma(s, xr[:, :, ta:ta + 512], xrt[:, ti % 2])
                        st2.append(cst[ti % 2].n)

                for en in ("scalar", "vector", "tensor", "sync"):
                    @getattr(blk, en)
                    def _(e):
                        cst[0].wait(e)
                        cst[1].wait(e)
            es2.close()
        barrier()

    def p4_phase(do_ffn):
        QT = 1024
        for q in range(T // QT):
            t0 = q * QT
            pfx = "p4q%d" % q
            with (
                nc.sbuf_tensor(pfx + "_zb", [128, NC, QT], BF16) as zbq,
                nc.sbuf_tensor(pfx + "_msk", [128, 8, NEXP], F32) as msk,
                nc.sbuf_tensor(pfx + "_ysb", [128, 8, D], BF16) as ysb,
            ):
                glg = sem("p4lg")
                if do_ffn:
                    with (
                        nc.sbuf_tensor(pfx + "_wg", [128, NC, 8], BF16) as wg_sb,
                        nc.sbuf_tensor(pfx + "_mx", [128, 1], F32) as mx,
                        nc.psum_tensor(pfx + "_pgt", [128, 2, 512], F32) as pgt,
                    ):
                        gmx = sem("p4mx"); gmk = sem("p4mk")
                        mk_snap = []

                        @blk.gpsimd
                        def _(g):
                            glg.dma(g, wg_sb[:],
                                    W["L1_ffn_Wg"].ap().rearrange(
                                        "(c p) e -> p c e", p=128))
                        glg_snap = glg.n

                        def gate_extra(i, ta, zsnap, pz):
                            for tk2 in range(2):
                                tko = (ta - t0) // 128 + tk2

                                @blk.tensor
                                def _(t, ta=ta, tk2=tk2, tko=tko, zsnap=zsnap, pz=pz):
                                    t.wait_ge(glg.h, glg_snap)
                                    t.wait_ge(pz.h, zsnap)
                                    if tko >= 2:
                                        gmk.wait(t, mk_snap[tko - 2])
                                    o = (ta - t0) + tk2 * 128
                                    for c in range(NC):
                                        li = t.matmul(pgt[:, tko % 2, 0:8],
                                                 zbq[:, c, o:o + 128],
                                                 wg_sb[:, c, :],
                                                 start=(c == 0),
                                                 stop=(c == NC - 1))
                                    gmx.inc(li)

                                @blk.vector
                                def _(v, tko=tko):
                                    gmx.wait(v)
                                    v.reduce_max(mx[:], pgt[:, tko % 2, 0:8],
                                                 axis=AX.X)
                                    gmk.inc(v.drain())
                                    v.wait_ge(gmk.h, gmk.n)
                                    for e in range(NEXP):
                                        v.tensor_tensor(msk[:, tko, e:e + 1],
                                                        pgt[:, tko % 2, e:e + 1],
                                                        mx[:], ALU.is_ge)
                                    gmk.inc(v.drain())
                                    mk_snap.append(gmk.n)

                        ln_pass(t0, t0 + QT, zbq, lambda i, ta, t0=t0: ta - t0,
                                extra=gate_extra)
                    _experts(q, t0, pfx, zbq, msk, ysb)
                else:
                    @blk.gpsimd
                    def _(g):
                        glg.inc(g.memset(ysb[:], 0.0))

                _exit_quarter(q, t0, pfx, ysb, do_ffn, glg)
        barrier()

    def _experts(q, t0, pfx, zbq, msk, ysb):
        lw = dsem("exl"); lw3 = dsem("ex3"); umm = sem("exu"); uge = sem("exg")
        vev = sem("exv"); ymm = sem("exm"); yev = sem("exy")
        gmx_f, gmk_f = sem("p4mx"), sem("p4mk")
        for en in ("gpsimd", "tensor", "sync"):
            @getattr(blk, en)
            def _(e):
                gmx_f.wait(e)
                gmk_f.wait(e)
        with (
            nc.sbuf_tensor(pfx + "_w1", [128, 2, NC, 512], BF16) as w1s,
            nc.sbuf_tensor(pfx + "_w2", [128, 2, NC, 512], BF16) as w2s,
            nc.sbuf_tensor(pfx + "_w3", [128, 2, 32, 512], BF16) as w3s,
            nc.sbuf_tensor(pfx + "_hb", [128, 32, 1024], BF16) as hb,
            nc.sbuf_tensor(pfx + "_gu", [128, 4, 512], BF16) as gub,
            nc.psum_tensor(pfx + "_pu", [128, 2, 512], F32) as puP,
            nc.psum_tensor(pfx + "_pv", [128, 2, 512], F32) as pvP,
            nc.psum_tensor(pfx + "_py", [128, 2, 512], F32) as pyP,
        ):
            w_snap = {}; w3_snap = {}; um_snap = []; ug_snap = []
            hv_snap = []; ym_snap = []; yv_snap = []
            for e in range(NEXP + 1):
                def w12src(mat, s8, e=e):
                    nm = ("L1_ffn_w1s", "L1_ffn_w2s")[mat] if e == 0 else \
                         ("L1_ffn_W1", "L1_ffn_W2")[mat]
                    ap = W[nm].ap() if e == 0 else W[nm][e - 1]
                    return ap.rearrange("(c p) o -> p c o", p=128)[
                        :, :, s8 * 512:(s8 + 1) * 512]

                def w3src(dq, e=e):
                    nm = "L1_ffn_w3s" if e == 0 else "L1_ffn_W3"
                    ap = W[nm].ap() if e == 0 else W[nm][e - 1]
                    return ap.rearrange("(rc p) o -> p rc o", p=128)[
                        :, :, dq * 512:(dq + 1) * 512]

                e_scalar = []
                for s8 in range(NC):
                    k = e * NC + s8

                    @blk.gpsimd
                    def _(g, s8=s8, k=k, e=e):
                        if k >= 2:
                            umm.wait(g, um_snap[(k - 1) * 8 - 1] if k >= 2
                                     else 0)
                        lw[s8 % 2].dma(g, w1s[:, s8 % 2], w12src(0, s8))
                        lw[s8 % 2].dma(g, w2s[:, s8 % 2], w12src(1, s8))
                        w_snap[k] = lw[s8 % 2].n

                    u0, g0, v0 = umm.n, uge.n, vev.n
                    jb = k * 8

                    @blk.tensor
                    def _(t, s8=s8, k=k, e=e, u0=u0, g0=g0, v0=v0, jb=jb):
                        t.wait_ge(lw[s8 % 2].h, w_snap[k])
                        if e > 0 and s8 == 0:
                            ymm.wait(t, ym_snap[-1])
                        for tt in range(2):
                            for m in range(4):
                                j = tt * 4 + m
                                if jb + j >= 2:
                                    t.wait_ge(uge.h, g0 + j - 1)
                                    t.wait_ge(vev.h, v0 + j - 1)
                                for c in range(NC):
                                    li = t.matmul(puP[:, m % 2],
                                             w1s[:, s8 % 2, c, m * 128:(m + 1) * 128],
                                             zbq[:, c, tt * 512:(tt + 1) * 512],
                                             start=(c == 0), stop=(c == NC - 1))
                                umm.inc(li)
                                for c in range(NC):
                                    li = t.matmul(pvP[:, m % 2],
                                             w2s[:, s8 % 2, c, m * 128:(m + 1) * 128],
                                             zbq[:, c, tt * 512:(tt + 1) * 512],
                                             start=(c == 0), stop=(c == NC - 1))
                                umm.inc(li)
                    for _j in range(8):
                        um_snap.append(u0 + 2 * _j + 2)
                    assert umm.n == u0 + 16

                    def ge_emit(s, u0=u0, v0=v0, jb=jb):
                        for tt in range(2):
                            for m in range(4):
                                j = tt * 4 + m
                                s.wait_ge(umm.h, u0 + 2 * j + 1)
                                if jb + j >= 2:
                                    s.wait_ge(vev.h, v0 + j - 1)
                                s.activation(gub[:, m, :], puP[:, m % 2],
                                             AF.Gelu).then_inc(uge.h, 1)
                    e_scalar.append(ge_emit)
                    uge.n += 8
                    assert uge.n == g0 + 8

                    @blk.vector
                    def _(v, s8=s8, u0=u0, g0=g0):
                        for tt in range(2):
                            for m in range(4):
                                j = tt * 4 + m
                                v.wait_ge(uge.h, g0 + j + 1)
                                v.wait_ge(umm.h, u0 + 2 * j + 2)
                                ic = s8 * 4 + m
                                vev.inc(v.tensor_tensor(
                                    hb[:, ic, tt * 512:(tt + 1) * 512],
                                    pvP[:, m % 2], gub[:, m, :], ALU.mult))
                    for _j in range(8):
                        hv_snap.append(v0 + _j + 1)
                    assert vev.n == v0 + 8

                @blk.scalar
                def _(s, e_scalar=e_scalar):
                    for fn in e_scalar:
                        fn(s)

                for dq in range(2):
                    k3 = e * 2 + dq

                    @blk.gpsimd
                    def _(g, dq=dq, k3=k3):
                        if k3 >= 2:
                            ymm.wait(g, ym_snap[(k3 - 1) * 8 - 1] if
                                     len(ym_snap) >= (k3 - 1) * 8 else 0)
                        lw3[dq % 2].dma(g, w3s[:, dq % 2], w3src(dq))
                        w3_snap[k3] = lw3[dq % 2].n

                    for tk in range(8):
                        j3 = k3 * 8 + tk

                        @blk.tensor
                        def _(t, dq=dq, tk=tk, k3=k3, j3=j3):
                            t.wait_ge(lw3[dq % 2].h, w3_snap[k3])
                            vev.wait(t)
                            if j3 >= 2:
                                yev.wait(t, yv_snap[j3 - 2])
                            for ic in range(32):
                                li = t.matmul(pyP[:, tk % 2],
                                         hb[:, ic, tk * 128:(tk + 1) * 128],
                                         w3s[:, dq % 2, ic, :],
                                         start=(ic == 0), stop=(ic == 31))
                            ymm.inc(li)
                            ym_snap.append(ymm.n)

                        @blk.vector
                        def _(v, e=e, dq=dq, tk=tk):
                            ymm.wait(v)
                            dsl = ysb[:, tk, dq * 512:(dq + 1) * 512]
                            if e == 0:
                                yev.inc(v.tensor_copy(dsl, pyP[:, tk % 2]))
                            else:
                                yev.inc(v.scalar_tensor_tensor(
                                    dsl, pyP[:, tk % 2], msk[:, tk, e - 1:e],
                                    dsl, ALU.mult, ALU.add))
                            yv_snap.append(yev.n)

            for en in ("tensor", "vector", "scalar", "sync", "gpsimd"):
                @getattr(blk, en)
                def _(eg):
                    yev.wait(eg)

    def _exit_quarter(q, t0, pfx, ysb, do_ffn, glg):
        ex = dsem("xtx"); em = sem("xtm"); evv = sem("xtv")
        eo = dsem("xto")
        with (
            nc.sbuf_tensor(pfx + "_ext", [128, 2, NC, 128], F32) as ext,
            nc.sbuf_tensor(pfx + "_eos", [128, 2, D], F32) as eos,
            nc.psum_tensor(pfx + "_ep", [128, 2, D], F32) as epx,
        ):
            xs3, ms3, vs3, os3 = [], [], [], []
            for tk in range(8):
                ta = t0 + tk * 128
                bb, s0 = ta // S, ta % S

                @blk.sync
                def _(sy, tk=tk, ta=ta):
                    if tk >= 2:
                        em.wait(sy, ms3[tk - 2])
                    ex[tk % 2].dma(sy, ext[:, tk % 2], xr[:, :, ta:ta + 128])
                    xs3.append(ex[tk % 2].n)

                @blk.tensor
                def _(t, tk=tk):
                    t.wait_ge(ex[tk % 2].h, xs3[tk])
                    if tk >= 2:
                        evv.wait(t, vs3[tk - 2])
                    for c in range(NC):
                        li = t.matmul(epx[:, tk % 2, c * 128:(c + 1) * 128],
                                 ext[:, tk % 2][:, c, :], id32_sb[:],
                                 start=(c % 4 == 0), stop=(c % 4 == 3),
                                 skip_group_check=True)
                    em.inc(li)
                    ms3.append(em.n)

                @blk.vector
                def _(v, tk=tk):
                    v.wait_ge(em.h, ms3[tk])
                    if not do_ffn:
                        glg.wait(v)
                    if tk >= 2:
                        eo[tk % 2].wait(v, os3[tk - 2])
                    evv.inc(v.tensor_tensor(eos[:, tk % 2], epx[:, tk % 2],
                                            ysb[:, tk, :], ALU.add))
                    vs3.append(evv.n)

                @blk.scalar
                def _(s, tk=tk, bb=bb, s0=s0):
                    s.wait_ge(evv.h, vs3[tk])
                    eo[tk % 2].dma(s, out_t[bb, s0:s0 + 128, :], eos[:, tk % 2])
                    os3.append(eo[tk % 2].n)

            for en in ("scalar", "sync", "vector", "tensor", "gpsimd"):
                @getattr(blk, en)
                def _(e):
                    eo[0].wait(e)
                    eo[1].wait(e)

    if blocks >= 1:
        attn_phase(0)
    if blocks >= 2:
        p2_phase()
    if blocks >= 3:
        attn_phase(1)
    p4_phase(do_ffn=(blocks >= 4))

    es.close()
    return nc


_prog_cache = {}


def _get_prog(blocks=4):
    if blocks not in _prog_cache:
        _prog_cache[blocks] = build_program(blocks)
    return _prog_cache[blocks]


def kernel(**inputs):
    import ml_dtypes
    nc = _get_prog(4)
    x = np.ascontiguousarray(np.asarray(inputs["L0_attn_x"], dtype=np.float32))
    id32v = np.eye(128, dtype=np.float32)
    idbfv = np.eye(128, dtype=ml_dtypes.bfloat16)
    in_maps = []
    for core in range(NCORES):
        m = {"x_in": x[core * SPC:(core + 1) * SPC], "id32": id32v,
             "idbf": idbfv}
        for k, vv in inputs.items():
            if k != "L0_attn_x":
                a = np.ascontiguousarray(np.asarray(vv, dtype=np.float32))
                if a.ndim == 0:
                    a = a.reshape(1)
                m[k] = a
        in_maps.append(m)
    res = run_bass_kernel_spmd(nc, in_maps, core_ids=list(range(NCORES)))
    return np.concatenate([r["out"] for r in res.results], axis=0)


# version 06
# speedup vs baseline: 1.1058x; 1.0929x over previous
import numpy as np
from contextlib import ExitStack

import concourse.bass as bass
import concourse.mybir as mybir
from concourse.bass_utils import run_bass_kernel_spmd

F32 = mybir.dt.float32
BF16 = mybir.dt.bfloat16
AF = mybir.ActivationFunctionType
ALU = mybir.AluOpType
AX = mybir.AxisListType

NCORES = 8
B, S, D = 16, 2048, 1024
SPC = B // NCORES
T = SPC * S
NC = D // 128
NH = 16
INTER = 4096
NEXP = 8
EPS = 1e-5


class Sem:
    def __init__(self, nc, name):
        self.h = nc.alloc_semaphore(name)
        self.n = 0

    def inc(self, instr, k=1):
        instr.then_inc(self.h, k)
        self.n += k
        return instr

    def dma(self, eng, out, in_):
        return self.inc(eng.dma_start(out=out, in_=in_), 16)

    def wait(self, eng, n=None):
        eng.wait_ge(self.h, self.n if n is None else n)


def build_program(blocks=4):
    nc = bass.Bass("TRN2", target_bir_lowering=False, debug=False,
                   detect_race_conditions=False)

    x_in = nc.dram_tensor("x_in", [SPC, S, D], F32, kind="ExternalInput")
    out_t = nc.dram_tensor("out", [SPC, S, D], F32, kind="ExternalOutput")
    id32 = nc.dram_tensor("id32", [128, 128], F32, kind="ExternalInput")
    idbf = nc.dram_tensor("idbf", [128, 128], BF16, kind="ExternalInput")

    W = {}
    def win(name, shape):
        W[name] = nc.dram_tensor(name, list(shape), F32, kind="ExternalInput")
    for n in ("q", "k", "v", "o"):
        win(f"L0_attn_w_{n}", (D, D))
        win(f"L0_attn_s_{n}", (1,))
        win(f"L0_attn_b_{n}", (D,))
    for i, (a, b) in enumerate([(D, D), (D, 512), (512, 256), (256, 1)]):
        win(f"L0_ffn_cn_w{i+1}", (a, b))
        win(f"L0_ffn_cn_b{i+1}", (b,))
    for i, (a, b) in enumerate([(D, 512), (512, 256), (256, 1)]):
        win(f"L0_ffn_gc_w{i+1}", (a, b))
        win(f"L0_ffn_gc_b{i+1}", (b,))
    win("L0_ffn_adaptive_range", (1,))
    for n in ("Wq", "Wk", "Wv"):
        win(f"L1_attn_{n}", (NH, D, 64))
    win("L1_attn_Wp", (D, D))
    win("L1_attn_bp", (D,))
    win("L1_ffn_w1s", (D, INTER))
    win("L1_ffn_w2s", (D, INTER))
    win("L1_ffn_w3s", (INTER, D))
    win("L1_ffn_W1", (NEXP, D, INTER))
    win("L1_ffn_W2", (NEXP, D, INTER))
    win("L1_ffn_W3", (NEXP, INTER, D))
    win("L1_ffn_Wg", (D, NEXP))

    xr = nc.dram_tensor("xr", [128, NC, T], F32, kind="Internal")
    wt0 = nc.dram_tensor("wt0", [4, 128, NC, D], BF16, kind="Internal")

    es = ExitStack()
    sbuf = lambda name, shape, dt: es.enter_context(nc.sbuf_tensor(name, shape, dt))

    id32_sb = sbuf("id32_sb", [128, 128], F32)
    idbf_sb = sbuf("idbf_sb", [128, 128], BF16)
    ones1 = sbuf("ones1", [1, 128], F32)
    invD = sbuf("invD", [128, 1], BF16)
    sclrw = sbuf("sclrw", [1, 8], F32)
    scl = sbuf("scl", [128, 8], F32)
    bQ = sbuf("bQ", [128, NC], F32)
    bK = sbuf("bK", [128, NC], F32)
    bV = sbuf("bV", [128, NC], F32)
    bO = sbuf("bO", [128, NC], F32)
    bP = sbuf("bP", [128, NC], F32)
    bv_row = sbuf("bv_row", [1, D], F32)
    bp_zero = sbuf("bp_zero", [128, 1], F32)
    b1s = sbuf("b1s", [128, 8], F32)
    b2s = sbuf("b2s", [128, 4], F32)
    b3s = sbuf("b3s", [128, 2], F32)
    gb1 = sbuf("gb1", [128, 4], F32)
    gb2 = sbuf("gb2", [128, 2], F32)
    rowc = sbuf("rowc", [1, 8], F32)
    cgrow = sbuf("cgrow", [1, 2], F32)
    srow2 = sbuf("srow2", [1, 4], F32)

    blk = es.enter_context(nc.Block())

    _sems = {}
    def sem(name):
        if name not in _sems:
            _sems[name] = Sem(nc, name)
        return _sems[name]

    def dsem(name, k=2):
        return [sem(name + chr(97 + j)) for j in range(k)]

    pb = sem("pb")
    _nbar = [0]

    def barrier():
        _nbar[0] += 1
        tgt = 5 * _nbar[0]
        for en in ("tensor", "vector", "scalar", "sync", "gpsimd"):
            @getattr(blk, en)
            def _(e):
                e.drain().then_inc(pb.h, 1)
                e.wait_ge(pb.h, tgt)
        pb.n = tgt

    ld0 = sem("ld0"); g0 = sem("g0"); mm0 = sem("mm0"); ev0 = sem("ev0")

    @blk.sync
    def _(sy):
        es_nc = nc.allow_non_contiguous_dma("tiny one-time bias/scalar loads")
        es_nc.__enter__()
        ld0.dma(sy, id32_sb[:], id32[:])
        ld0.dma(sy, idbf_sb[:], idbf[:])
        for i, n in enumerate(("q", "k", "v", "o")):
            ld0.dma(sy, sclrw[0:1, i:i + 1], W[f"L0_attn_s_{n}"].ap()[None, :])
        ld0.dma(sy, sclrw[0:1, 4:5], W["L0_ffn_adaptive_range"].ap()[None, :])
        for dst, src in ((bQ, "L0_attn_b_q"), (bK, "L0_attn_b_k"),
                         (bV, "L0_attn_b_v"), (bO, "L0_attn_b_o"),
                         (bP, "L1_attn_bp")):
            ld0.dma(sy, dst[:], W[src].ap().rearrange("(c p) -> p c", p=128))
        ld0.dma(sy, bv_row[0:1, :], W["L0_attn_b_v"].ap()[None, :])
        ld0.dma(sy, b1s[:], W["L0_ffn_cn_b1"].ap().rearrange("(c p) -> p c", p=128))
        ld0.dma(sy, b2s[:], W["L0_ffn_cn_b2"].ap().rearrange("(c p) -> p c", p=128))
        ld0.dma(sy, b3s[:], W["L0_ffn_cn_b3"].ap().rearrange("(c p) -> p c", p=128))
        ld0.dma(sy, gb1[:], W["L0_ffn_gc_b1"].ap().rearrange("(c p) -> p c", p=128))
        ld0.dma(sy, gb2[:], W["L0_ffn_gc_b2"].ap().rearrange("(c p) -> p c", p=128))
        ld0.dma(sy, rowc[0:1, 0:1], W["L0_ffn_cn_b4"].ap()[None, :])
        ld0.dma(sy, rowc[0:1, 1:2], W["L0_ffn_gc_b3"].ap()[None, :])
        es_nc.__exit__(None, None, None)

    @blk.gpsimd
    def _(g):
        g.memset(ones1[:], 1.0)
        g.memset(bp_zero[:], 0.0)
        g.memset(rowc[0:1, 6:7], EPS)
        g0.inc(g.memset(invD[:], 1.0 / 1024.0))

    with nc.psum_tensor("p0scl", [128, 8], F32) as p0scl:
        @blk.tensor
        def _(t):
            ld0.wait(t)
            g0.wait(t)
            mm0.inc(t.matmul(p0scl[:, 0:5], ones1[:], sclrw[0:1, 0:5], start=True, stop=True))

        @blk.vector
        def _(v):
            mm0.wait(v)
            v.tensor_copy(scl[:, 0:5], p0scl[:, 0:5])
            v.tensor_scalar(rowc[0:1, 2:3], sclrw[0:1, 4:5], 0.3, None, ALU.mult)
            v.tensor_scalar(rowc[0:1, 3:4], rowc[0:1, 0:1], 0.5, None, ALU.mult)
            v.tensor_scalar(rowc[0:1, 4:5], rowc[0:1, 1:2], 0.5, None, ALU.mult)
            ev0.inc(v.tensor_scalar(rowc[0:1, 5:6], sclrw[0:1, 4:5], 0.7, None,
                                    ALU.mult))

    with (
        nc.sbuf_tensor("e_xt", [128, 2, D], F32) as e_xt,
        nc.sbuf_tensor("e_fm", [128, 2, D], F32) as e_fm,
        nc.psum_tensor("e_p", [128, 2, D], F32) as e_p,
    ):
        ldx = dsem("p0ldx"); mmx = sem("p0mmx"); evx = sem("p0evx"); stx = dsem("p0stx")
        ld_s, mm_s, ev_s, st_s = [], [], [], []
        for i in range(T // 128):
            bb, s0 = (i * 128) // S, (i * 128) % S

            @blk.sync
            def _(sy, i=i, bb=bb, s0=s0):
                if i >= 2:
                    mmx.wait(sy, mm_s[i - 2])
                ldx[i % 2].dma(sy, e_xt[:, i % 2], x_in[bb, s0:s0 + 128, :])
                ld_s.append(ldx[i % 2].n)

            @blk.tensor
            def _(t, i=i):
                if i == 0:
                    ev0.wait(t)
                t.wait_ge(ldx[i % 2].h, ld_s[i])
                if i >= 2:
                    evx.wait(t, ev_s[i - 2])
                for c in range(NC):
                    li = t.matmul(e_p[:, i % 2, c * 128:(c + 1) * 128],
                             e_xt[:, i % 2][:, c * 128:(c + 1) * 128], id32_sb[:],
                             start=(c % 4 == 0), stop=(c % 4 == 3),
                             skip_group_check=True)
                mmx.inc(li)
                mm_s.append(mmx.n)

            @blk.vector
            def _(v, i=i):
                v.wait_ge(mmx.h, mm_s[i])
                if i >= 2:
                    stx[i % 2].wait(v, st_s[i - 2])
                evx.inc(v.tensor_copy(e_fm[:, i % 2], e_p[:, i % 2]))
                ev_s.append(evx.n)

            @blk.scalar
            def _(s, i=i):
                s.wait_ge(evx.h, ev_s[i])
                stx[i % 2].dma(s, xr[:, :, i * 128:(i + 1) * 128],
                        e_fm[:, i % 2].rearrange("p (c t) -> p c t", c=NC))
                st_s.append(stx[i % 2].n)

        @blk.scalar
        def _(s):
            stx[0].wait(s)
            stx[1].wait(s)

    barrier()

    if blocks >= 1:
        with (
            nc.sbuf_tensor("w_tmp", [128, 2, NC, D], BF16) as w_tmp,
            nc.sbuf_tensor("w_tsb", [128, 2, D], BF16) as w_tsb,
            nc.psum_tensor("w_p", [128, 2, D], F32) as w_p,
        ):
            ldw = dsem("p0ldw"); mmw = sem("p0mmw"); evw = sem("p0evw"); stw = dsem("p0stw")
            lds, mms, evs, sts = [], [], [], []
            for w, nme in enumerate(("q", "k", "v", "o")):
                @blk.gpsimd
                def _(g, w=w, nme=nme):
                    if w >= 2:
                        mmw.wait(g, mms[(w - 1) * NC - 1])
                    ldw[w % 2].dma(g, w_tmp[:, w % 2],
                            W[f"L0_attn_w_{nme}"].ap().rearrange(
                                "(r p) h -> p r h", p=128))
                    lds.append(ldw[w % 2].n)

                for r in range(NC):
                    gi = w * NC + r

                    @blk.tensor
                    def _(t, w=w, r=r, gi=gi):
                        t.wait_ge(ldw[w % 2].h, lds[w])
                        if gi >= 2:
                            evw.wait(t, evs[gi - 2])
                        for c in range(NC):
                            li = t.matmul(w_p[:, gi % 2, c * 128:(c + 1) * 128],
                                     w_tmp[:, w % 2, r, c * 128:(c + 1) * 128],
                                     idbf_sb[:],
                                     start=(c % 4 == 0), stop=(c % 4 == 3),
                                     skip_group_check=True)
                        mmw.inc(li)
                        mms.append(mmw.n)

                    @blk.vector
                    def _(v, gi=gi):
                        v.wait_ge(mmw.h, mms[gi])
                        if gi >= 2:
                            stw[gi % 2].wait(v, sts[gi - 2])
                        evw.inc(v.tensor_copy(w_tsb[:, gi % 2], w_p[:, gi % 2]))
                        evs.append(evw.n)

                    @blk.scalar
                    def _(s, w=w, r=r, gi=gi):
                        s.wait_ge(evw.h, evs[gi])
                        stw[gi % 2].dma(s, wt0[w, :, :, r * 128:(r + 1) * 128],
                                w_tsb[:, gi % 2].rearrange("p (c o) -> p c o", c=NC))
                        sts.append(stw[gi % 2].n)

            @blk.scalar
            def _(s):
                stw[0].wait(s)
                stw[1].wait(s)

    barrier()

    _lnid = [0]

    def ln_pass(t0, t1, zb, zb_off_fn, store_rows=None, extra=None):
        _lnid[0] += 1
        pfx = "ln"
        px = dsem(pfx + "x"); pc = sem(pfx + "c"); pm = sem(pfx + "m")
        pq = sem(pfx + "q"); pv = sem(pfx + "v"); pr = sem(pfx + "r")
        pm2 = sem(pfx + "b"); pz = sem(pfx + "z")
        n = (t1 - t0) // 256
        ufx = "ln%d" % _lnid[0]
        with (
            nc.sbuf_tensor(ufx + "_xt", [128, 2, NC, 256], F32) as xt,
            nc.sbuf_tensor(ufx + "_xb", [128, NC, 256], BF16) as xb,
            nc.sbuf_tensor(ufx + "_xq", [128, NC, 256], BF16) as xq,
            nc.sbuf_tensor(ufx + "_zt", [128, 256], F32) as zt,
            nc.sbuf_tensor(ufx + "_rw", [1, 1024], F32) as rw,
            nc.psum_tensor(ufx + "_ps", [1, 2, 512], F32) as ps,
            nc.psum_tensor(ufx + "_pb", [128, 2, 512], F32) as pbx,
        ):
            px0 = [px[0].n, px[1].n]
            pc0, pm0, pq0, pv0, pr0, pb0, pz0 = (pc.n, pm.n, pq.n, pv.n,
                                                 pr.n, pm2.n, pz.n)
            zs = []
            for i in range(n):
                ta = t0 + i * 256

                @blk.sync
                def _(sy, i=i, ta=ta):
                    if i >= 2:
                        sy.wait_ge(pz.h, pz0 + i - 1)
                    px[i % 2].dma(sy, xt[:, i % 2], xr[:, :, ta:ta + 256])

                @blk.scalar
                def _(s, i=i):
                    s.wait_ge(px[i % 2].h, px0[i % 2] + 16 * (i // 2 + 1))
                    if i >= 1:
                        s.wait_ge(pm.h, pm0 + i)
                        s.wait_ge(pm2.h, pb0 + i)
                        if store_rows is not None:
                            s.wait_ge(pz.h, pz0 + i)
                    s.activation(xb[:], xt[:, i % 2], AF.Identity)
                    pc.inc(s.activation(xq[:], xt[:, i % 2], AF.Square))
                    s.wait_ge(pm.h, pm0 + i + 1)
                    s.activation(rw[0:1, 0:256], ps[0:1, i % 2, 0:256], AF.Identity)
                    pq.inc(s.activation(rw[0:1, 512:768], ps[0:1, i % 2, 0:256],
                                        AF.Square))
                    s.wait_ge(pv.h, pv0 + i + 1)
                    pr.inc(s.activation(rw[0:1, 768:1024], rw[0:1, 768:1024],
                                        AF.Ln, bias=rowc[0:1, 6:7]))
                    s.wait_ge(pr.h, pr.n)
                    pr.inc(s.activation(rw[0:1, 256:512], rw[0:1, 768:1024],
                                        AF.Exp, scale=-0.5))

                @blk.tensor
                def _(t, i=i):
                    t.wait_ge(pc.h, pc0 + i + 1)
                    if i >= 2:
                        t.wait_ge(pv.h, pv0 + i - 1)
                    for c in range(NC):
                        t.matmul(ps[0:1, i % 2, 0:256], invD[:], xb[:, c, :],
                                 start=(c == 0), stop=(c == NC - 1),
                                 skip_group_check=True)
                    for c in range(NC):
                        li = t.matmul(ps[0:1, i % 2, 256:512], invD[:], xq[:, c, :],
                                 start=False, stop=(c == NC - 1),
                                 skip_group_check=True)
                    pm.inc(li)
                    t.wait_ge(pr.h, pr0 + 2 * i + 2)
                    if i >= 2:
                        t.wait_ge(pz.h, pz0 + i - 1)
                    pm2.inc(t.matmul(pbx[:, i % 2], ones1[:], rw[0:1, 0:512],
                             start=True, stop=True, skip_group_check=True))

                @blk.vector
                def _(v, i=i, ta=ta):
                    v.wait_ge(pq.h, pq0 + i + 1)
                    pv.inc(v.tensor_tensor(rw[0:1, 768:1024],
                                           ps[0:1, i % 2, 256:512],
                                           rw[0:1, 512:768], ALU.subtract))
                    v.wait_ge(pm2.h, pb0 + i + 1)
                    zo = zb_off_fn(i, ta)
                    for c in range(NC):
                        v.tensor_tensor(zt[:], xt[:, i % 2][:, c, :],
                                        pbx[:, i % 2][:, 0:256], ALU.subtract)
                        v.tensor_tensor(zb[:, c, zo:zo + 256], zt[:],
                                        pbx[:, i % 2][:, 256:512], ALU.mult)
                    if store_rows is not None:
                        mrow, rrow = store_rows
                        v.tensor_copy(mrow[0:1, ta - t0:ta - t0 + 256],
                                      rw[0:1, 0:256])
                        v.tensor_copy(rrow[0:1, ta - t0:ta - t0 + 256],
                                      rw[0:1, 256:512])
                    pz.inc(v.drain())
                zs.append(pz0 + i + 1)

                if extra is not None:
                    extra(i, ta, zs[i], pz)

            for en in ("vector", "tensor", "scalar", "sync", "gpsimd"):
                @getattr(blk, en)
                def _(e):
                    pz.wait(e)
        return pz

    def attn_phase(layer):
        Lp = "a%d" % layer
        with (
            nc.sbuf_tensor(Lp + "_Q", [128, NC, S], BF16) as Qs,
            nc.sbuf_tensor(Lp + "_K", [128, NC, S], BF16) as Ks,
            nc.sbuf_tensor(Lp + "_V", [128, 16, NH, 65], BF16) as Vs,
            nc.sbuf_tensor(Lp + "_bvsb", [128, D], F32) as bv_sb,
            nc.sbuf_tensor(Lp + "_wo", [128, NC, D], BF16) as wo_sb,
        ):
            lwo = sem("aplwo%d" % layer)
            if layer == 0:
                @blk.sync
                def _(sy):
                    lwo.dma(sy, wo_sb[:], wt0[3])
            else:
                @blk.gpsimd
                def _(g):
                    wpv = W["L1_attn_Wp"].ap().rearrange(
                        "(e c a) o -> e c a o", e=64, c=8, a=2)
                    for aa in range(2):
                        lwo.dma(g, wo_sb[aa * 64:(aa + 1) * 64, :, :],
                                wpv[:, :, aa, :])
            lwo_snap = lwo.n

            if layer == 0:
                pvb = sem("apvb")
                with nc.psum_tensor(Lp + "_pvb", [128, D], F32) as p_vb:
                    @blk.tensor
                    def _(t):
                        ev0.wait(t)
                        for half in range(2):
                            li = t.matmul(p_vb[:, half * 512:(half + 1) * 512],
                                     ones1[:],
                                     bv_row[0:1, half * 512:(half + 1) * 512],
                                     start=True, stop=True,
                                     skip_group_check=True)
                        pvb.inc(li)

                    @blk.vector
                    def _(v):
                        pvb.wait(v)
                        pvb.inc(v.tensor_copy(bv_sb[:], p_vb[:]))

            gv = sem("apgv")

            @blk.gpsimd
            def _(g):
                gv.inc(g.memset(Vs[:, :, :, 64:65], 1.0))

            if layer == 0:
                @blk.tensor
                def _(t):
                    pvb.wait(t)

            for seq in range(SPC):
                t0 = seq * S
                with nc.sbuf_tensor(Lp + "_zb%d" % seq, [128, NC, S], BF16) as zb:
                    pz = ln_pass(t0, t0 + S, zb, lambda i, ta, t0=t0: ta - t0)
                    _proj(layer, seq, zb, Qs, Ks, Vs, bv_sb, pz, gv)
                _attn_core(layer, seq, t0, Qs, Ks, Vs, wo_sb, lwo, lwo_snap)
        barrier()

    def _proj(layer, seq, zb, Qs, Ks, Vs, bv_sb, pz, gv):
        Lp = "a%dp%d" % (layer, seq)
        lw = dsem("prlw%d" % layer); pmm = sem("prmm"); pev = sem("prev")
        with (
            nc.sbuf_tensor(Lp + "_w", [128, 2, NC, D], BF16) as w_sb,
            nc.psum_tensor(Lp + "_pp", [128, 2, 512], F32) as pp,
        ):
            lws = []
            wdone = []
            evl = []
            slot = [0]

            for wi in range(3):
                eng = "sync" if layer == 0 else "gpsimd"

                @getattr(blk, eng)
                def _(e, wi=wi):
                    if wi >= 2:
                        pmm.wait(e, wdone[wi - 2])
                    if layer == 0:
                        lw[wi % 2].dma(e, w_sb[:, wi % 2], wt0[wi])
                    else:
                        nm = ("Wq", "Wk", "Wv")[wi]
                        wv4 = W[f"L1_attn_{nm}"].ap().rearrange(
                            "h (c p) e -> c p h e", p=128)
                        for c in range(NC):
                            lw[wi % 2].dma(
                                e,
                                w_sb[:, wi % 2, c, :].rearrange(
                                    "p (h e) -> p h e", e=64),
                                wv4[c])
                    lws.append(lw[wi % 2].n)

                if wi < 2:
                    dst = (Qs, Ks)[wi]
                    for ts in range(S // 512):
                        for m in range(NC):
                            j = slot[0]

                            @blk.tensor
                            def _(t, wi=wi, ts=ts, m=m, j=j):
                                t.wait_ge(lw[wi % 2].h, lws[wi])
                                pz.wait(t)
                                if j >= 2:
                                    pev.wait(t, evl[j - 2])
                                for c in range(NC):
                                    li = t.matmul(pp[:, j % 2],
                                             w_sb[:, wi % 2][:, c, m * 128:(m + 1) * 128],
                                             zb[:, c, ts * 512:(ts + 1) * 512],
                                             start=(c == 0), stop=(c == NC - 1))
                                pmm.inc(li)

                            @blk.vector
                            def _(v, wi=wi, ts=ts, m=m, j=j):
                                pmm.wait(v)
                                dd = dst[:, m, ts * 512:(ts + 1) * 512]
                                if layer == 0:
                                    pev.inc(v.tensor_scalar(
                                        dd, pp[:, j % 2], scl[:, wi:wi + 1],
                                        (bQ, bK)[wi][:, m:m + 1],
                                        ALU.mult, ALU.add))
                                else:
                                    pev.inc(v.tensor_copy(dd, pp[:, j % 2]))
                                evl.append(pev.n)

                            slot[0] += 1
                    wdone.append(pmm.n)
                else:
                    for tk in range(S // 128):
                        for vc in range(2):
                            j = slot[0]

                            @blk.tensor
                            def _(t, tk=tk, vc=vc, j=j):
                                t.wait_ge(lw[0].h, lws[2])
                                pz.wait(t)
                                gv.wait(t)
                                if j >= 2:
                                    pev.wait(t, evl[j - 2])
                                for c in range(NC):
                                    li = t.matmul(pp[:, j % 2],
                                             zb[:, c, tk * 128:(tk + 1) * 128],
                                             w_sb[:, 0, c, vc * 512:(vc + 1) * 512],
                                             start=(c == 0), stop=(c == NC - 1))
                                pmm.inc(li)

                            @blk.vector
                            def _(v, layer=layer, seq=seq, tk=tk, vc=vc, j=j):
                                pmm.wait(v)
                                dsl = Vs[:, tk, vc * 8:(vc + 1) * 8, 0:64]
                                src = pp[:, j % 2].rearrange("p (h e) -> p h e", e=64)
                                if layer == 0:
                                    bvv = bv_sb[:, vc * 512:(vc + 1) * 512] \
                                        .rearrange("p (h e) -> p h e", e=64)
                                    v.scalar_tensor_tensor(dsl, src, scl[:, 2:3],
                                                           bvv, ALU.mult, ALU.add)
                                else:
                                    v.tensor_copy(dsl, src)
                                pev.inc(v.drain())
                                evl.append(pev.n)

                            slot[0] += 1
                    wdone.append(pmm.n)

            for en in ("tensor", "vector", "scalar", "sync", "gpsimd"):
                @getattr(blk, en)
                def _(e):
                    pev.wait(e)

    def _attn_core(layer, seq, t0, Qs, Ks, Vs, wo_sb, lwo, lwo_snap):
        Lp = "a%dc%d" % (layer, seq)
        amm = sem("acmm"); aex = sem("acex"); acp = sem("accp")
        arw = sem("acrw"); abc = sem("acbc"); aev = sem("acev")
        aom = sem("acom"); alr = dsem("aclr"); aoe = sem("acoe")
        axv = sem("acxv"); axs = dsem("acxs")
        with (
            nc.sbuf_tensor(Lp + "_Pp", [128, 2, 1024], BF16) as Pp,
            nc.sbuf_tensor(Lp + "_ctx", [128, NC, 512], BF16) as ctx,
            nc.sbuf_tensor(Lp + "_rin", [64, 2, 512], F32) as rin,
            nc.sbuf_tensor(Lp + "_srow", [1, 1024], F32) as srow,
            nc.sbuf_tensor(Lp + "_xrt", [128, 2, NC, 512], F32) as xrt,
            nc.sbuf_tensor(Lp + "_otmp", [128, 2, 512], F32) as otmp,
            nc.psum_tensor(Lp + "_st", [128, 2, 1024], F32) as stp,
            nc.psum_tensor(Lp + "_cx", [65, 2, 512], F32) as cxp,
            nc.psum_tensor(Lp + "_rb", [64, 512], F32) as rbp,
            nc.psum_tensor(Lp + "_po", [128, 512], F32) as pop,
        ):
            rw_snap = {}; bc_snap = {}; rcopy = {}; ev_done = {}; bc_plan = {}
            rcopy_plan = {}; rw_plan = {}
            om_snap = []; oe_snap = []; xv_snap = []; st_snap = []; lr_snap = []
            pend_bc = [None]
            for qb in range(4):
                @blk.sync
                def _(sy, qb=qb):
                    if qb >= 2:
                        axs[qb % 2].wait(sy, st_snap[qb - 2])
                    alr[qb % 2].dma(sy, xrt[:, qb % 2],
                            xr[:, :, t0 + qb * 512: t0 + (qb + 1) * 512])
                    lr_snap.append(alr[qb % 2].n)

                qb_scalar = []
                for h in range(NH):
                    hp, ho = h // 2, (h % 2) * 64
                    gidx = qb * NH + h
                    amm0, aex0, acp0 = amm.n, aex.n, acp.n

                    @blk.tensor
                    def _(t, qb=qb, h=h, hp=hp, ho=ho, gidx=gidx,
                          aex0=aex0, acp0=acp0):
                        for g in range(8):
                            if gidx * 8 + g >= 2:
                                t.wait_ge(aex.h, aex0 + g - 1 if g >= 2
                                          else ex_tail[gidx - 1] - (1 - g))
                            for kt in (2 * g, 2 * g + 1):
                                li = t.matmul(
                                    stp[:, g % 2, (kt % 2) * 512:(kt % 2 + 1) * 512],
                                    Ks[ho:ho + 64, hp, kt * 128:(kt + 1) * 128],
                                    Qs[ho:ho + 64, hp, qb * 512:(qb + 1) * 512],
                                    start=True, stop=True, skip_group_check=True)
                            amm.inc(li)
                            if g >= 1:
                                t.wait_ge(aex.h, aex0 + g)
                                if g == 1 and gidx >= 2:
                                    aev.wait(t, ev_done[gidx - 2])
                                for kt in (2 * g - 2, 2 * g - 1):
                                    li = t.matmul(cxp[:, gidx % 2], Vs[:, kt, h, :],
                                             Pp[:, (g - 1) % 2, (kt % 2) * 512:(kt % 2 + 1) * 512],
                                             start=(kt == 0), stop=False,
                                             skip_group_check=True)
                                acp.inc(li)
                        t.wait_ge(aex.h, aex0 + 8)
                        for kt in (14, 15):
                            li = t.matmul(cxp[:, gidx % 2], Vs[:, kt, h, :],
                                     Pp[:, 1, (kt % 2) * 512:(kt % 2 + 1) * 512],
                                     start=False, stop=(kt == 15),
                                     skip_group_check=True)
                        acp.inc(li)
                        if pend_bc[0] is not None:
                            pend_bc[0](t)
                            pend_bc[0] = None
                    assert amm.n == amm0 + 8 and acp.n == acp0 + 8

                    def sc_emit(s, gidx=gidx, amm0=amm0, aex0=aex0, acp0=acp0):
                        for g in range(8):
                            s.wait_ge(amm.h, amm0 + g + 1)
                            if gidx * 8 + g >= 2:
                                s.wait_ge(acp.h, acp0 + g - 1 if g >= 2
                                          else cp_tail[gidx - 1] - (1 - g))
                            s.activation(Pp[:, g % 2], stp[:, g % 2],
                                         AF.Exp, scale=0.125).then_inc(aex.h, 1)
                        s.wait_ge(acp.h, acp0 + 8)
                        if gidx >= 1:
                            s.wait_ge(abc.h, bc_plan[gidx - 1])
                        s.activation(srow[0:1, 0:512], cxp[64:65, gidx % 2],
                                     AF.Ln).then_inc(arw.h, 1)
                        s.wait_ge(arw.h, rw_plan[gidx] - 1)
                        s.activation(srow[0:1, 512:1024], srow[0:1, 0:512],
                                     AF.Exp, scale=-1.0).then_inc(arw.h, 1)
                    qb_scalar.append(sc_emit)
                    aex.n += 8
                    rw_plan[gidx] = arw.n + 2
                    arw.n += 2
                    rw_snap[gidx] = rw_plan[gidx]
                    ex_tail = getattr(_attn_core, "_ex_tail", {})
                    ex_tail[gidx] = aex0 + 8
                    _attn_core._ex_tail = ex_tail
                    cp_tail = getattr(_attn_core, "_cp_tail", {})
                    cp_tail[gidx] = acp0 + 8
                    _attn_core._cp_tail = cp_tail

                    def mk_bc(gidx=gidx):
                        def emit(t):
                            t.wait_ge(arw.h, rw_snap[gidx])
                            if gidx >= 1:
                                t.wait_ge(aev.h, rcopy_plan[gidx - 1])
                            t.matmul(rbp[:], ones1[0:1, 0:64],
                                     srow[0:1, 512:1024], start=True, stop=True,
                                     skip_group_check=True).then_inc(abc.h, 1)
                        return emit
                    pend_bc[0] = mk_bc()
                    bc_plan[gidx] = abc.n + 1
                    abc.n += 1

                    @blk.vector
                    def _(v, qb=qb, hp=hp, ho=ho, gidx=gidx):
                        v.wait_ge(abc.h, bc_plan[gidx])
                        if qb >= 1 and gidx % NH == 0:
                            aom.wait(v, om_snap[qb - 1])
                        v.tensor_copy(rin[:, gidx % 2], rbp[:])
                        aev.inc(v.drain())
                        rcopy_plan[gidx] = aev.n
                        v.wait_ge(aev.h, aev.n)
                        aev.inc(v.tensor_tensor(ctx[ho:ho + 64, hp, :],
                                                cxp[0:64, gidx % 2],
                                                rin[:, gidx % 2], ALU.mult))
                        ev_done[gidx] = aev.n

                @blk.scalar
                def _(s, qb_scalar=qb_scalar):
                    for fn in qb_scalar:
                        fn(s)

                @blk.tensor
                def _(t):
                    if pend_bc[0] is not None:
                        pend_bc[0](t)
                        pend_bc[0] = None
                for m in range(NC):
                    j = qb * NC + m

                    @blk.tensor
                    def _(t, qb=qb, m=m, j=j):
                        t.wait_ge(lwo.h, lwo_snap)
                        aev.wait(t)
                        if j >= 1:
                            aoe.wait(t, oe_snap[j - 1])
                        for c in range(NC):
                            li = t.matmul(pop[:],
                                     wo_sb[:, c, m * 128:(m + 1) * 128],
                                     ctx[:, c, :],
                                     start=(c == 0), stop=(c == NC - 1))
                        aom.inc(li)
                        if m == NC - 1:
                            om_snap.append(aom.n)

                    @blk.vector
                    def _(v, qb=qb, m=m, j=j):
                        aom.wait(v)
                        if j >= 2:
                            axv.wait(v, xv_snap[j - 2])
                        if layer == 0:
                            bias, scale = bO[:, m:m + 1], scl[:, 3:4]
                        else:
                            bias, scale = bP[:, m:m + 1], 1.0
                        aoe.inc(v.tensor_scalar(otmp[:, j % 2], pop[:],
                                                scale, bias, ALU.mult, ALU.add))
                        oe_snap.append(aoe.n)
                        v.wait_ge(alr[qb % 2].h, lr_snap[qb])
                        axv.inc(v.tensor_tensor(xrt[:, qb % 2, m, :],
                                                xrt[:, qb % 2, m, :],
                                                otmp[:, j % 2], ALU.add))
                        xv_snap.append(axv.n)

                @blk.scalar
                def _(s, qb=qb):
                    axv.wait(s)
                    axs[qb % 2].dma(s, xr[:, :, t0 + qb * 512: t0 + (qb + 1) * 512],
                            xrt[:, qb % 2])
                    st_snap.append(axs[qb % 2].n)

            for en in ("scalar", "vector", "tensor", "sync"):
                @getattr(blk, en)
                def _(e):
                    axs[0].wait(e)
                    axs[1].wait(e)

    def p2_phase():
        with (
            nc.sbuf_tensor("p2_mrow", [1, T], F32) as mrow,
            nc.sbuf_tensor("p2_rrow", [1, T], F32) as rrow,
            nc.sbuf_tensor("p2_zbt", [128, NC, 512], BF16) as zbt,
            nc.sbuf_tensor("p2_gacc", [128, NC, 2], F32) as gacc,
            nc.sbuf_tensor("p2_gtmp", [128, NC], F32) as gtmp,
            nc.sbuf_tensor("p2_cn1", [128, NC, D], BF16) as cn1,
            nc.sbuf_tensor("p2_cn2", [128, NC, 512], BF16) as cn2,
            nc.sbuf_tensor("p2_cn3", [128, 4, 256], BF16) as cn3,
            nc.sbuf_tensor("p2_cn4", [128, 2, 1], BF16) as cn4,
            nc.sbuf_tensor("p2_gc1", [128, NC, 512], BF16) as gc1,
            nc.sbuf_tensor("p2_gc2", [128, 4, 256], BF16) as gc2,
            nc.sbuf_tensor("p2_gc3", [128, 2, 1], BF16) as gc3,
        ):
            lw2 = sem("p2lw"); gz = sem("p2gz"); gv2 = sem("p2gv")

            @blk.gpsimd
            def _(g):
                gz.inc(g.memset(gacc[:], 0.0))
                with nc.allow_non_contiguous_dma("tiny [.,1] weight columns"):
                    for nm, dst in (("cn_w1", cn1), ("cn_w2", cn2),
                                    ("cn_w3", cn3), ("cn_w4", cn4),
                                    ("gc_w1", gc1), ("gc_w2", gc2),
                                    ("gc_w3", gc3)):
                        lw2.dma(g, dst[:], W["L0_ffn_" + nm].ap().rearrange(
                            "(c p) o -> p c o", p=128))

            def gc_extra(i, ta, zsnap, pz):
                @blk.vector
                def _(v, i=i, ta=ta, zsnap=zsnap):
                    gz.wait(v)
                    sq = ta // S
                    off = (i % 2) * 256
                    v.tensor_reduce(gtmp[:], zbt[:, :, off:off + 256],
                                    AX.X, ALU.add)
                    gv2.inc(v.tensor_tensor(gacc[:, :, sq], gacc[:, :, sq],
                                            gtmp[:], ALU.add))

            ln_pass(0, T, zbt, lambda i, ta: (i % 2) * 256,
                    store_rows=(mrow, rrow), extra=gc_extra)

            gmm = sem("p2gmm"); gev = sem("p2gev")
            with (
                nc.sbuf_tensor("p2_gcb", [128, NC, 2], BF16) as gcb,
                nc.sbuf_tensor("p2_g1", [128, 4, 2], BF16) as g1b,
                nc.sbuf_tensor("p2_g2", [128, 2, 2], BF16) as g2b,
                nc.psum_tensor("p2_pg", [128, 8], F32) as pg,
                nc.psum_tensor("p2_pg2", [128, 4], F32) as pg2,
                nc.psum_tensor("p2_pg3", [1, 2], F32) as pg3,
            ):
                @blk.vector
                def _(v):
                    gv2.wait(v)
                    gev.inc(v.tensor_scalar(gcb[:], gacc[:], 1.0 / S, None,
                                            ALU.mult))

                @blk.tensor
                def _(t):
                    gev.wait(t)
                    lw2.wait(t)
                    for m in range(4):
                        for c in range(NC):
                            li = t.matmul(pg[:, m * 2:(m + 1) * 2],
                                     gc1[:, c, m * 128:(m + 1) * 128],
                                     gcb[:, c, :],
                                     start=(c == 0 and m == 0),
                                     stop=(c == NC - 1 and m == 3),
                                     skip_group_check=True)
                    gmm.inc(li)

                @blk.scalar
                def _(s):
                    gmm.wait(s)
                    for m in range(4):
                        s.activation(g1b[:, m, :], pg[:, m * 2:(m + 1) * 2],
                                     AF.Gelu, bias=gb1[:, m:m + 1])
                    gev.inc(s.drain())

                @blk.tensor
                def _(t):
                    gev.wait(t)
                    for m in range(2):
                        for c in range(4):
                            li = t.matmul(pg2[:, m * 2:(m + 1) * 2],
                                     gc2[:, c, m * 128:(m + 1) * 128],
                                     g1b[:, c, :],
                                     start=(c == 0 and m == 0),
                                     stop=(c == 3 and m == 1),
                                     skip_group_check=True)
                    gmm.inc(li)

                @blk.scalar
                def _(s):
                    gmm.wait(s)
                    for m in range(2):
                        s.activation(g2b[:, m, :], pg2[:, m * 2:(m + 1) * 2],
                                     AF.Gelu, bias=gb2[:, m:m + 1])
                    gev.inc(s.drain())

                @blk.tensor
                def _(t):
                    gev.wait(t)
                    for c in range(2):
                        li = t.matmul(pg3[:], gc3[:, c, :], g2b[:, c, :],
                                 start=(c == 0), stop=(c == 1),
                                 skip_group_check=True)
                    gmm.inc(li)

                @blk.scalar
                def _(s):
                    gmm.wait(s)
                    gev.inc(s.activation(srow2[0:1, 0:2], pg3[0:1, :], AF.Tanh,
                                         scale=0.5, bias=rowc[0:1, 4:5]))

                @blk.vector
                def _(v):
                    gev.wait(v)
                    gev.inc(v.tensor_scalar(cgrow[0:1, :], srow2[0:1, 0:2],
                                            rowc[0:1, 5:6], 0.1,
                                            ALU.mult, ALU.add))

            cx = dsem("p2cx"); cb = sem("p2cb"); cmm = sem("p2cmm")
            cge = sem("p2cge"); ctf = sem("p2ctf"); ccr = sem("p2ccr")
            cbc = sem("p2cbc"); cup = sem("p2cup"); cst = dsem("p2cst")
            cca = sem("p2cca")
            SGN = [1.0, 1.0, -1.0, -1.0, -1.0, -1.0, 1.0, 1.0]
            PRT = [4, 5, 6, 7, 0, 1, 2, 3]
            es2 = ExitStack()
            xrt = es2.enter_context(nc.sbuf_tensor("p2_xrt", [128, 2, NC, 512], F32))
            oA = es2.enter_context(nc.sbuf_tensor("p2_oA", [128, NC, 512], F32))
            oB = es2.enter_context(nc.sbuf_tensor("p2_oB", [128, NC, 512], F32))
            ob = es2.enter_context(nc.sbuf_tensor("p2_ob", [128, NC, 512], BF16))
            u1b = es2.enter_context(nc.sbuf_tensor("p2_u1", [128, NC, 512], BF16))
            u2b = es2.enter_context(nc.sbuf_tensor("p2_u2", [128, 4, 512], BF16))
            u3b = es2.enter_context(nc.sbuf_tensor("p2_u3", [128, 2, 512], BF16))
            csb = es2.enter_context(nc.sbuf_tensor("p2_csb", [128, 512], F32))
            ztm = es2.enter_context(nc.sbuf_tensor("p2_ztm", [128, 512], F32))
            trow = es2.enter_context(nc.sbuf_tensor("p2_trow", [1, 512], F32))
            crow = es2.enter_context(nc.sbuf_tensor("p2_crow", [1, 512], F32))
            pb2 = es2.enter_context(nc.psum_tensor("p2_pb2", [128, 2, 1024], F32))
            pu = es2.enter_context(nc.psum_tensor("p2_pu", [128, 2, 512], F32))
            p4 = es2.enter_context(nc.psum_tensor("p2_p4", [1, 512], F32))
            pcr = es2.enter_context(nc.psum_tensor("p2_pcr", [128, 512], F32))
            if True:
                xs2, st2 = [], []
                cc0 = cca.n

                @blk.tensor
                def _(t):
                    gev.wait(t)
                @blk.sync
                def _(sy):
                    gev.wait(sy)

                def dense_layer(win_sb, bsb, uout, nm_, nk_, cycsem):
                    c0, g0 = cmm.n, cge.n

                    @blk.tensor
                    def _(t):
                        cge.wait(t, g0)
                        for m in range(nm_):
                            if m >= 2:
                                t.wait_ge(cge.h, g0 + m - 1)
                            for c in range(nk_):
                                li = t.matmul(pu[:, m % 2],
                                         win_sb[:, c, m * 128:(m + 1) * 128],
                                         cycsem[0][:, c, :],
                                         start=(c == 0), stop=(c == nk_ - 1))
                            cmm.inc(li)

                    @blk.scalar
                    def _(s):
                        for m in range(nm_):
                            s.wait_ge(cmm.h, c0 + m + 1)
                            cge.inc(s.activation(uout[:, m, :], pu[:, m % 2],
                                                 AF.Gelu, bias=bsb[:, m:m + 1]))

                for ti in range(T // 512):
                    ta = ti * 512
                    sq = ta // S

                    @blk.sync
                    def _(sy, ti=ti, ta=ta):
                        if ti >= 2:
                            cst[ti % 2].wait(sy, st2[ti - 2])
                        cx[ti % 2].dma(sy, xrt[:, ti % 2], xr[:, :, ta:ta + 512])
                        xs2.append(cx[ti % 2].n)

                    @blk.tensor
                    def _(t, ti=ti, ta=ta):
                        t.wait_ge(cx[ti % 2].h, xs2[ti])
                        if ti >= 2:
                            cca.wait(t, cc0 + ti - 1)
                        t.matmul(pb2[:, ti % 2, 0:512], ones1[:],
                                 mrow[0:1, ta:ta + 512],
                                 start=True, stop=True, skip_group_check=True)
                        cb.inc(t.matmul(pb2[:, ti % 2, 512:1024], ones1[:],
                                 rrow[0:1, ta:ta + 512],
                                 start=True, stop=True, skip_group_check=True))

                    @blk.vector
                    def _(v, ti=ti):
                        cb.wait(v)
                        for c in range(NC):
                            v.tensor_tensor(ztm[:], xrt[:, ti % 2, c, :],
                                            pb2[:, ti % 2, 0:512], ALU.subtract)
                            v.tensor_tensor(oA[:, c, :], ztm[:],
                                            pb2[:, ti % 2, 512:1024], ALU.mult)
                            v.tensor_copy(ob[:, c, :], oA[:, c, :])
                        cup.inc(v.drain())
                        cge.inc(v.drain())

                    for cyc in range(3):
                        src, dst = (oA, oB) if cyc % 2 == 0 else (oB, oA)
                        dense_layer(cn1, b1s, u1b, NC, NC, [ob])
                        dense_layer(cn2, b2s, u2b, 4, NC, [u1b])
                        dense_layer(cn3, b3s, u3b, 2, 4, [u2b])

                        @blk.tensor
                        def _(t):
                            cge.wait(t)
                            for c in range(2):
                                li = t.matmul(p4[:], cn4[:, c, :], u3b[:, c, :],
                                         start=(c == 0), stop=(c == 1),
                                         skip_group_check=True)
                            cmm.inc(li)

                        @blk.scalar
                        def _(s):
                            cmm.wait(s)
                            ctf.inc(s.activation(trow[0:1, :], p4[0:1, :],
                                                 AF.Tanh, scale=0.5,
                                                 bias=rowc[0:1, 3:4]))

                        @blk.vector
                        def _(v, sq=sq):
                            ctf.wait(v)
                            ccr.inc(v.tensor_scalar(crow[0:1, :], trow[0:1, :],
                                                    rowc[0:1, 2:3],
                                                    cgrow[0:1, sq:sq + 1],
                                                    ALU.mult, ALU.add))

                        @blk.tensor
                        def _(t):
                            ccr.wait(t)
                            cbc.inc(t.matmul(pcr[:], ones1[:], crow[0:1, :],
                                     start=True, stop=True,
                                     skip_group_check=True))

                        @blk.vector
                        def _(v, cyc=cyc, src=src, dst=dst):
                            cbc.wait(v)
                            v.tensor_copy(csb[:], pcr[:])
                            cup.inc(v.drain())
                            v.wait_ge(cup.h, cup.n)
                            for c in range(NC):
                                v.tensor_tensor(ztm[:], csb[:],
                                                src[:, PRT[c], :], ALU.mult)
                                v.scalar_tensor_tensor(dst[:, c, :], ztm[:],
                                                       SGN[c], src[:, c, :],
                                                       ALU.mult, ALU.add)
                                if cyc < 2:
                                    v.tensor_copy(ob[:, c, :], dst[:, c, :])
                            cup.inc(v.drain())
                            if cyc < 2:
                                cge.inc(v.drain())

                    @blk.vector
                    def _(v, ti=ti):
                        cup.wait(v)
                        for c in range(NC):
                            v.tensor_tensor(xrt[:, ti % 2, c, :],
                                            xrt[:, ti % 2, c, :],
                                            oB[:, c, :], ALU.add)
                        cca.inc(v.drain())

                    @blk.scalar
                    def _(s, ti=ti, ta=ta):
                        cca.wait(s)
                        cst[ti % 2].dma(s, xr[:, :, ta:ta + 512], xrt[:, ti % 2])
                        st2.append(cst[ti % 2].n)

                for en in ("scalar", "vector", "tensor", "sync"):
                    @getattr(blk, en)
                    def _(e):
                        cst[0].wait(e)
                        cst[1].wait(e)
            es2.close()
        barrier()

    def p4_phase(do_ffn):
        QT = 1024
        for q in range(T // QT):
            t0 = q * QT
            pfx = "p4q%d" % q
            with (
                nc.sbuf_tensor(pfx + "_zb", [128, NC, QT], BF16) as zbq,
                nc.sbuf_tensor(pfx + "_msk", [128, 8, NEXP], F32) as msk,
                nc.sbuf_tensor(pfx + "_ysb", [128, 8, D], BF16) as ysb,
            ):
                glg = sem("p4lg")
                if do_ffn:
                    with (
                        nc.sbuf_tensor(pfx + "_wg", [128, NC, 8], BF16) as wg_sb,
                        nc.sbuf_tensor(pfx + "_mx", [128, 1], F32) as mx,
                        nc.psum_tensor(pfx + "_pgt", [128, 2, 512], F32) as pgt,
                    ):
                        gmx = sem("p4mx"); gmk = sem("p4mk")
                        mk_snap = []

                        @blk.gpsimd
                        def _(g):
                            glg.dma(g, wg_sb[:],
                                    W["L1_ffn_Wg"].ap().rearrange(
                                        "(c p) e -> p c e", p=128))
                        glg_snap = glg.n

                        def gate_extra(i, ta, zsnap, pz):
                            for tk2 in range(2):
                                tko = (ta - t0) // 128 + tk2

                                @blk.tensor
                                def _(t, ta=ta, tk2=tk2, tko=tko, zsnap=zsnap, pz=pz):
                                    t.wait_ge(glg.h, glg_snap)
                                    t.wait_ge(pz.h, zsnap)
                                    if tko >= 2:
                                        gmk.wait(t, mk_snap[tko - 2])
                                    o = (ta - t0) + tk2 * 128
                                    for c in range(NC):
                                        li = t.matmul(pgt[:, tko % 2, 0:8],
                                                 zbq[:, c, o:o + 128],
                                                 wg_sb[:, c, :],
                                                 start=(c == 0),
                                                 stop=(c == NC - 1))
                                    gmx.inc(li)

                                @blk.vector
                                def _(v, tko=tko):
                                    gmx.wait(v)
                                    v.reduce_max(mx[:], pgt[:, tko % 2, 0:8],
                                                 axis=AX.X)
                                    gmk.inc(v.drain())
                                    v.wait_ge(gmk.h, gmk.n)
                                    for e in range(NEXP):
                                        v.tensor_tensor(msk[:, tko, e:e + 1],
                                                        pgt[:, tko % 2, e:e + 1],
                                                        mx[:], ALU.is_ge)
                                    gmk.inc(v.drain())
                                    mk_snap.append(gmk.n)

                        ln_pass(t0, t0 + QT, zbq, lambda i, ta, t0=t0: ta - t0,
                                extra=gate_extra)
                    _experts(q, t0, pfx, zbq, msk, ysb)
                else:
                    @blk.gpsimd
                    def _(g):
                        glg.inc(g.memset(ysb[:], 0.0))

                _exit_quarter(q, t0, pfx, ysb, do_ffn, glg)
        barrier()

    def _experts(q, t0, pfx, zbq, msk, ysb):
        lw = dsem("exl"); lw3 = dsem("ex3"); umm = sem("exu"); uge = sem("exg")
        vev = sem("exv"); ymm = sem("exm"); yev = sem("exy")
        gmx_f, gmk_f = sem("p4mx"), sem("p4mk")
        for en in ("gpsimd", "tensor", "sync"):
            @getattr(blk, en)
            def _(e):
                gmx_f.wait(e)
                gmk_f.wait(e)
        with (
            nc.sbuf_tensor(pfx + "_w1", [128, 2, NC, 512], BF16) as w1s,
            nc.sbuf_tensor(pfx + "_w2", [128, 2, NC, 512], BF16) as w2s,
            nc.sbuf_tensor(pfx + "_w3", [128, 2, 32, 512], BF16) as w3s,
            nc.sbuf_tensor(pfx + "_hb", [128, 32, 1024], BF16) as hb,
            nc.sbuf_tensor(pfx + "_gu", [128, 4, 512], BF16) as gub,
            nc.psum_tensor(pfx + "_pu", [128, 2, 512], F32) as puP,
            nc.psum_tensor(pfx + "_pv", [128, 2, 512], F32) as pvP,
            nc.psum_tensor(pfx + "_py", [128, 2, 512], F32) as pyP,
        ):
            w_snap = {}; w3_snap = {}; um_snap = []; ug_snap = []
            hv_snap = []; ym_snap = []; yv_snap = []
            for e in range(NEXP + 1):
                def w12src(mat, s8, e=e):
                    nm = ("L1_ffn_w1s", "L1_ffn_w2s")[mat] if e == 0 else \
                         ("L1_ffn_W1", "L1_ffn_W2")[mat]
                    ap = W[nm].ap() if e == 0 else W[nm][e - 1]
                    return ap.rearrange("(c p) o -> p c o", p=128)[
                        :, :, s8 * 512:(s8 + 1) * 512]

                def w3src(dq, e=e):
                    nm = "L1_ffn_w3s" if e == 0 else "L1_ffn_W3"
                    ap = W[nm].ap() if e == 0 else W[nm][e - 1]
                    return ap.rearrange("(rc p) o -> p rc o", p=128)[
                        :, :, dq * 512:(dq + 1) * 512]

                e_scalar = []
                for s8 in range(NC):
                    k = e * NC + s8

                    @blk.gpsimd
                    def _(g, s8=s8, k=k, e=e):
                        if k >= 2:
                            umm.wait(g, um_snap[(k - 1) * 8 - 1] if k >= 2
                                     else 0)
                        lw[s8 % 2].dma(g, w1s[:, s8 % 2], w12src(0, s8))
                        lw[s8 % 2].dma(g, w2s[:, s8 % 2], w12src(1, s8))
                        w_snap[k] = lw[s8 % 2].n

                    u0, g0, v0 = umm.n, uge.n, vev.n
                    jb = k * 8

                    @blk.tensor
                    def _(t, s8=s8, k=k, e=e, u0=u0, g0=g0, v0=v0, jb=jb):
                        t.wait_ge(lw[s8 % 2].h, w_snap[k])
                        if e > 0 and s8 == 0:
                            ymm.wait(t, ym_snap[-1])
                        for tt in range(2):
                            for m in range(4):
                                j = tt * 4 + m
                                if jb + j >= 2:
                                    t.wait_ge(uge.h, g0 + j - 1)
                                    t.wait_ge(vev.h, v0 + j - 1)
                                for c in range(NC):
                                    li = t.matmul(puP[:, m % 2],
                                             w1s[:, s8 % 2, c, m * 128:(m + 1) * 128],
                                             zbq[:, c, tt * 512:(tt + 1) * 512],
                                             start=(c == 0), stop=(c == NC - 1))
                                umm.inc(li)
                                for c in range(NC):
                                    li = t.matmul(pvP[:, m % 2],
                                             w2s[:, s8 % 2, c, m * 128:(m + 1) * 128],
                                             zbq[:, c, tt * 512:(tt + 1) * 512],
                                             start=(c == 0), stop=(c == NC - 1))
                                umm.inc(li)
                    for _j in range(8):
                        um_snap.append(u0 + 2 * _j + 2)
                    assert umm.n == u0 + 16

                    def ge_emit(s, u0=u0, v0=v0, jb=jb):
                        for tt in range(2):
                            for m in range(4):
                                j = tt * 4 + m
                                s.wait_ge(umm.h, u0 + 2 * j + 1)
                                if jb + j >= 2:
                                    s.wait_ge(vev.h, v0 + j - 1)
                                s.activation(gub[:, m, :], puP[:, m % 2],
                                             AF.Gelu).then_inc(uge.h, 1)
                    e_scalar.append(ge_emit)
                    uge.n += 8
                    assert uge.n == g0 + 8

                    @blk.vector
                    def _(v, s8=s8, u0=u0, g0=g0):
                        for tt in range(2):
                            for m in range(4):
                                j = tt * 4 + m
                                v.wait_ge(uge.h, g0 + j + 1)
                                v.wait_ge(umm.h, u0 + 2 * j + 2)
                                ic = s8 * 4 + m
                                vev.inc(v.tensor_tensor(
                                    hb[:, ic, tt * 512:(tt + 1) * 512],
                                    pvP[:, m % 2], gub[:, m, :], ALU.mult))
                    for _j in range(8):
                        hv_snap.append(v0 + _j + 1)
                    assert vev.n == v0 + 8

                @blk.scalar
                def _(s, e_scalar=e_scalar):
                    for fn in e_scalar:
                        fn(s)

                for dq in range(2):
                    k3 = e * 2 + dq

                    @blk.gpsimd
                    def _(g, dq=dq, k3=k3):
                        if k3 >= 2:
                            ymm.wait(g, ym_snap[(k3 - 1) * 8 - 1] if
                                     len(ym_snap) >= (k3 - 1) * 8 else 0)
                        lw3[dq % 2].dma(g, w3s[:, dq % 2], w3src(dq))
                        w3_snap[k3] = lw3[dq % 2].n

                    for tk in range(8):
                        j3 = k3 * 8 + tk

                        @blk.tensor
                        def _(t, dq=dq, tk=tk, k3=k3, j3=j3):
                            t.wait_ge(lw3[dq % 2].h, w3_snap[k3])
                            vev.wait(t)
                            if j3 >= 2:
                                yev.wait(t, yv_snap[j3 - 2])
                            for ic in range(32):
                                li = t.matmul(pyP[:, tk % 2],
                                         hb[:, ic, tk * 128:(tk + 1) * 128],
                                         w3s[:, dq % 2, ic, :],
                                         start=(ic == 0), stop=(ic == 31))
                            ymm.inc(li)
                            ym_snap.append(ymm.n)

                        @blk.vector
                        def _(v, e=e, dq=dq, tk=tk):
                            ymm.wait(v)
                            dsl = ysb[:, tk, dq * 512:(dq + 1) * 512]
                            if e == 0:
                                yev.inc(v.tensor_copy(dsl, pyP[:, tk % 2]))
                            else:
                                yev.inc(v.scalar_tensor_tensor(
                                    dsl, pyP[:, tk % 2], msk[:, tk, e - 1:e],
                                    dsl, ALU.mult, ALU.add))
                            yv_snap.append(yev.n)

            for en in ("tensor", "vector", "scalar", "sync", "gpsimd"):
                @getattr(blk, en)
                def _(eg):
                    yev.wait(eg)

    def _exit_quarter(q, t0, pfx, ysb, do_ffn, glg):
        ex = dsem("xtx"); em = sem("xtm"); evv = sem("xtv")
        eo = dsem("xto")
        with (
            nc.sbuf_tensor(pfx + "_ext", [128, 2, NC, 128], F32) as ext,
            nc.sbuf_tensor(pfx + "_eos", [128, 2, D], F32) as eos,
            nc.psum_tensor(pfx + "_ep", [128, 2, D], F32) as epx,
        ):
            xs3, ms3, vs3, os3 = [], [], [], []
            for tk in range(8):
                ta = t0 + tk * 128
                bb, s0 = ta // S, ta % S

                @blk.sync
                def _(sy, tk=tk, ta=ta):
                    if tk >= 2:
                        em.wait(sy, ms3[tk - 2])
                    ex[tk % 2].dma(sy, ext[:, tk % 2], xr[:, :, ta:ta + 128])
                    xs3.append(ex[tk % 2].n)

                @blk.tensor
                def _(t, tk=tk):
                    t.wait_ge(ex[tk % 2].h, xs3[tk])
                    if tk >= 2:
                        evv.wait(t, vs3[tk - 2])
                    for c in range(NC):
                        li = t.matmul(epx[:, tk % 2, c * 128:(c + 1) * 128],
                                 ext[:, tk % 2][:, c, :], id32_sb[:],
                                 start=(c % 4 == 0), stop=(c % 4 == 3),
                                 skip_group_check=True)
                    em.inc(li)
                    ms3.append(em.n)

                @blk.vector
                def _(v, tk=tk):
                    v.wait_ge(em.h, ms3[tk])
                    if not do_ffn:
                        glg.wait(v)
                    if tk >= 2:
                        eo[tk % 2].wait(v, os3[tk - 2])
                    evv.inc(v.tensor_tensor(eos[:, tk % 2], epx[:, tk % 2],
                                            ysb[:, tk, :], ALU.add))
                    vs3.append(evv.n)

                @blk.scalar
                def _(s, tk=tk, bb=bb, s0=s0):
                    s.wait_ge(evv.h, vs3[tk])
                    eo[tk % 2].dma(s, out_t[bb, s0:s0 + 128, :], eos[:, tk % 2])
                    os3.append(eo[tk % 2].n)

            for en in ("scalar", "sync", "vector", "tensor", "gpsimd"):
                @getattr(blk, en)
                def _(e):
                    eo[0].wait(e)
                    eo[1].wait(e)

    if blocks >= 1:
        attn_phase(0)
    if blocks >= 2:
        p2_phase()
    if blocks >= 3:
        attn_phase(1)
    p4_phase(do_ffn=(blocks >= 4))

    es.close()
    return nc


_prog_cache = {}


def _get_prog(blocks=4):
    if blocks not in _prog_cache:
        _prog_cache[blocks] = build_program(blocks)
    return _prog_cache[blocks]


def kernel(**inputs):
    import ml_dtypes
    nc = _get_prog(4)
    x = np.ascontiguousarray(np.asarray(inputs["L0_attn_x"], dtype=np.float32))
    id32v = np.eye(128, dtype=np.float32)
    idbfv = np.eye(128, dtype=ml_dtypes.bfloat16)
    in_maps = []
    for core in range(NCORES):
        m = {"x_in": x[core * SPC:(core + 1) * SPC], "id32": id32v,
             "idbf": idbfv}
        for k, vv in inputs.items():
            if k != "L0_attn_x":
                a = np.ascontiguousarray(np.asarray(vv, dtype=np.float32))
                if a.ndim == 0:
                    a = a.reshape(1)
                m[k] = a
        in_maps.append(m)
    res = run_bass_kernel_spmd(nc, in_maps, core_ids=list(range(NCORES)))
    return np.concatenate([r["out"] for r in res.results], axis=0)
